# version 02
# baseline (speedup 1.0000x reference)
import os
import numpy as np
import ml_dtypes
import concourse.bass as bass
import concourse.mybir as mybir
from concourse.bass_utils import run_bass_kernel_spmd
from concourse.bass import IndirectOffsetOnAxis

F32 = mybir.dt.float32
BF16 = mybir.dt.bfloat16
I32 = mybir.dt.int32
AF = mybir.ActivationFunctionType
ALU = mybir.AluOpType

NB = 2
SEQ = 2048
NTOK = NB * SEQ
D = 1024
NH = 16
HD = 64
MLP = 4096
EI = 2048
EXP = 8
FF = 2048
CL = 1536
NSLOT = EXP * CL
CCAP = 10240
KEPS = 1e-3
LNEPS_IN = 1e-6
LNEPS_OUT = 1e-5
NCHUNK = NTOK // 128

ENGINES = ("tensor", "vector", "scalar", "gpsimd", "sync")


class Rec:
    __slots__ = ("eng", "fn", "waits", "inc_n", "seq", "label", "_value", "slot", "force")

    def __init__(self, eng, fn, waits, seq, label):
        self.eng = eng
        self.fn = fn
        self.waits = list(waits)
        self.inc_n = 0
        self.seq = seq
        self.label = label
        self._value = None
        self.slot = None
        self.force = False


class Tok:
    __slots__ = ("rec",)

    def __init__(self, rec):
        self.rec = rec


class Prog:

    def __init__(self, nc):
        self.nc = nc
        self.items = []
        self.slot_counts = {}
        self.slot_last = {}
        self.last_fn = {e: None for e in ENGINES}

    def op(self, eng, fn, waits=(), label=""):
        flat = []
        def _fl(w):
            if w is None:
                return
            if isinstance(w, Tok):
                flat.append(w)
            else:
                for x in w:
                    _fl(x)
        _fl(waits)
        r = Rec(eng, fn, flat, len(self.items), label)
        self.items.append(r)
        if fn is not None:
            self.last_fn[eng] = r
        return r

    def tok(self, rec, n=1):
        rec.inc_n = max(rec.inc_n, n)
        return Tok(rec)

    def dop(self, eng, fn, waits=(), label="", slot=None):
        r = self.op(eng, fn, waits, label)
        r.slot = slot if slot is not None else label
        r.inc_n = 16
        self.slot_counts[r.slot] = self.slot_counts.get(r.slot, 0) + 16
        r._value = self.slot_counts[r.slot]
        t = Tok(r)
        self.slot_last[r.slot] = (t, eng)
        return t

    def barrier(self, label="bar"):
        per_eng = {e: [] for e in ENGINES}
        for sl, (t, eng) in self.slot_last.items():
            per_eng[eng].append(t)
        self.slot_last = {}
        toks = []
        for e in ENGINES:
            w = list(per_eng[e])
            if self.last_fn[e] is not None and self.last_fn[e].slot is None:
                w.append(self.tok(self.last_fn[e]))
            r = self.op(e, None, w, label=f"{label}.{e}")
            r.force = True
            toks.append(self.tok(r, 1))
        for e in ENGINES:
            self.op(e, None, toks, label=f"{label}.w.{e}")

    CHAIN = ("vector", "scalar", "gpsimd")

    def finalize_and_emit(self, block, sems, slot_sems):
        for r in self.items:
            if r.slot is None and r.eng in self.CHAIN and r.fn is not None:
                r.inc_n = max(r.inc_n, 1)
        counts = {e: 0 for e in ENGINES}
        for r in self.items:
            if r.slot is None and r.inc_n:
                counts[r.eng] += r.inc_n
                r._value = counts[r.eng]

        def emitter(eng_name):
            def emit(eng):
                hiwater = {}
                prev_val = 0
                for r in self.items:
                    if r.eng != eng_name:
                        continue
                    need = {}
                    if eng_name in self.CHAIN and r.fn is not None and prev_val:
                        need[eng_name] = (prev_val, sems[eng_name])
                    for w in r.waits:
                        wr = w.rec
                        assert wr.inc_n > 0, f"wait on token without inc: {wr.label}"
                        if wr.slot is None:
                            if wr.eng == eng_name and wr.seq < r.seq and not r.force:
                                continue
                            key, sem = wr.eng, sems[wr.eng]
                        else:
                            key, sem = ("sl", wr.slot), slot_sems[wr.slot]
                        v = wr._value
                        if v > need.get(key, (0, None))[0]:
                            need[key] = (v, sem)
                    for key, (v, sem) in need.items():
                        if v > hiwater.get(key, 0):
                            eng.wait_ge(sem, v)
                            hiwater[key] = v
                    inst = r.fn(eng) if r.fn is not None else None
                    if r.inc_n:
                        sem = slot_sems[r.slot] if r.slot is not None else sems[r.eng]
                        if inst is not None:
                            inst.then_inc(sem, r.inc_n)
                        else:
                            eng.sem_inc(sem, r.inc_n)
                    if r.slot is None and r.inc_n and r.fn is not None:
                        prev_val = r._value
            return emit

        for e in ENGINES:
            getattr(block, e)(emitter(e))


class Banks:
    def __init__(self, ps):
        self.ps = ps
        self.free = {i: () for i in range(len(ps))}

    def get(self, i):
        return self.ps[i], self.free[i]

    def done(self, i, *toks):
        self.free[i] = tuple(toks)


def host_consts(core_id):
    c = {}
    c["c_texcl"] = np.triu(np.ones((128, 128), np.float32), 1)
    c["c_ones_col"] = np.ones((128, 1), np.float32)
    c["c_ones_row"] = np.ones((1, 128), np.float32)
    c["c_iota8"] = np.tile(np.arange(8, dtype=np.float32), (128, 1))
    c["c_capcol"] = (NSLOT + np.arange(128, dtype=np.float32))[:, None]
    h = np.zeros((1, 32), np.float32)
    h[0, 2 * core_id] = 1.0
    h[0, 16 + 2 * core_id + 1] = 1.0
    c["c_hsel"] = h
    m = np.zeros((16, 2), np.float32)
    m[0:2 * core_id:2, 0] = 1.0
    m[0::2, 1] = 1.0
    m[1:2 * core_id:2, 1] = 1.0
    c["c_msel"] = m
    c["c_onesbf"] = np.ones((128, 1), np.float32)
    return c


def host_weight_consts(inputs):
    c = {}
    t = lambda v: np.tile(np.asarray(v, np.float32).reshape(-1)[None, :], (128, 1)).astype(ml_dtypes.bfloat16)
    col = lambda v, n: np.ascontiguousarray(np.asarray(v, np.float32).reshape(n, 128).T)
    c["w_ln1g"] = t(inputs["L0_attn_a_ln1_scale"])
    c["w_ln1b"] = t(inputs["L0_attn_a_ln1_bias"])
    c["w_ln2g"] = t(inputs["L0_attn_a_ln2_scale"])
    c["w_ln2b"] = t(inputs["L0_attn_a_ln2_bias"])
    c["w_bk"] = t(inputs["L0_attn_a_bk"])
    c["w_bv"] = t(inputs["L0_attn_a_bv"])
    c["w_bo"] = t(inputs["L0_attn_a_bo"])
    c["w_b2"] = t(inputs["L0_attn_a_b2"])
    c["w_moutb"] = t(inputs["L0_attn_b_out_b"])
    c["w_bq"] = col(inputs["L0_attn_a_bq"], 8)
    c["w_b1"] = col(inputs["L0_attn_a_b1"], 32)
    c["w_inpb"] = col(inputs["L0_attn_b_in_proj_b"], 32)
    c["w_dtb"] = col(inputs["L0_attn_b_dt_b"], 16)
    c["w_alog"] = col(inputs["L0_attn_b_A_log"], 16)
    c["w_dpar"] = col(inputs["L0_attn_b_D_param"], 16)
    c["w_convb"] = col(inputs["L0_attn_b_conv_b"], 16)
    cw = np.asarray(inputs["L0_attn_b_conv_w"], np.float32).reshape(16, 128, 4)
    c["w_convw"] = np.ascontiguousarray(cw.transpose(1, 0, 2).reshape(128, 64))
    return c


class B:
    def __init__(self, nc):
        self.nc = nc
        self.p = Prog(nc)
        self.g = {}
        self.sb = {}
        self.ps = []
        self.st = {}

    def din(self, name, shape, dt):
        self.g[name] = self.nc.dram_tensor(name, list(shape), dt, kind="ExternalInput").ap()

    def dout(self, name, shape, dt):
        self.g[name] = self.nc.dram_tensor(name, list(shape), dt, kind="ExternalOutput").ap()

    def dscr(self, name, shape, dt, **kw):
        self.g[name] = self.nc.dram_tensor(name, list(shape), dt, kind="Internal", **kw).ap()

    def sbuf(self, name, shape, dt):
        self.sb[name] = self.nc.alloc_sbuf_tensor(name, list(shape), dt)
        return self.sb[name]


def declare(b):
    nc = b.nc
    b.din("xs", (NTOK, D), F32)
    for n, shp in [
        ("wq", (D, D)), ("wk", (D, D)), ("wv", (D, D)), ("wo", (D, D)),
        ("w1", (D, MLP)), ("w2", (MLP, D)),
        ("inpw", (2 * EI, D)), ("dtw", (EI, EI)), ("bw", (EI, EI)), ("cw", (EI, EI)),
        ("outw", (D, EI)),
        ("wg", (D, EXP)), ("wgin", (EXP, D, 2 * FF)), ("wgout", (EXP, FF, D)),
    ]:
        b.din(n, shp, F32)
    for n, shp in [
        ("c_texcl", (128, 128)), ("c_ones_col", (128, 1)), ("c_ones_row", (1, 128)),
        ("c_iota8", (128, 8)), ("c_capcol", (128, 1)), ("c_hsel", (1, 32)), ("c_msel", (16, 2)), ("c_onesbf", (128, 1)),
        ("w_bq", (128, 8)), ("w_b1", (128, 32)), ("w_inpb", (128, 32)), ("w_dtb", (128, 16)),
        ("w_alog", (128, 16)), ("w_dpar", (128, 16)), ("w_convb", (128, 16)), ("w_convw", (128, 64)),
    ]:
        b.din(n, shp, F32)
    for n in ["w_ln1g", "w_ln1b", "w_ln2g", "w_ln2b", "w_bk", "w_bv", "w_bo", "w_b2", "w_moutb"]:
        b.din(n, (128, D), BF16)
    b.dout("out", (NTOK, D), F32)

    b.dscr("RES", (NTOK, D), BF16)
    b.dscr("LNX", (NTOK, D), BF16)
    b.dscr("LNO", (NTOK, D), BF16)
    b.dscr("XRES", (NTOK, D), BF16)
    b.dscr("ATTN", (NTOK, D), BF16)
    b.dscr("ZSCR", (EI, NTOK), BF16)
    b.dscr("XBUF", (NSLOT + 128, D), BF16)
    b.dscr("OBUF", (NSLOT + 128, D), BF16)
    b.dscr("WS_inp", (2 * EI, D), BF16)
    b.dscr("WS_dt", (EI, EI), BF16)
    b.dscr("WS_b", (EI, EI), BF16)
    b.dscr("WS_c", (EI, EI), BF16)
    b.dscr("WS_out", (D, EI), BF16)
    for i in range(2):
        b.dscr(f"CCIN{i}", (16, 8), F32)
        b.dscr(f"CCOUT{i}", (16, 8), F32)
    b.dscr("DBG", (128, 1024), F32)

    for n, shp, dt in [
        ("texcl", (128, 128), F32), ("ones_col", (128, 1), F32), ("ones_row", (1, 128), F32),
        ("iota8", (128, 8), F32), ("capcol", (128, 1), F32), ("hsel", (1, 32), F32), ("msel", (16, 2), F32),
        ("bq", (128, 8), F32), ("b1", (128, 32), F32), ("inpb", (128, 32), F32), ("dtb", (128, 16), F32),
        ("alog", (128, 16), F32), ("dpar", (128, 16), F32), ("convb", (128, 16), F32), ("convw", (128, 64), F32),
        ("ln1g", (128, D), BF16), ("ln1b", (128, D), BF16), ("ln2g", (128, D), BF16), ("ln2b", (128, D), BF16),
        ("bkt", (128, D), BF16), ("bvt", (128, D), BF16), ("bot", (128, D), BF16), ("b2t", (128, D), BF16),
        ("moutbt", (128, D), BF16),
        ("acol", (128, 16), F32),
        ("epsc", (128, 4), F32),
    ]:
        b.sbuf(n, shp, dt)

    b.sbuf("aw0", (128, 32768), BF16)
    b.sbuf("aw1", (128, 32768), BF16)
    b.sbuf("am", (128, 16384), BF16)
    b.sbuf("stream", (128, 6144), BF16)
    b.sbuf("stat", (128, 64), F32)
    b.sbuf("route", (128, 320), F32)
    b.sbuf("ridx", (128, 128), I32)
    b.sbuf("small", (128, 512), F32)
    b.sbuf("kvs", (128, 1024), BF16)
    b.sbuf("wqslot", (128, 256), BF16)
    b.sbuf("onesb", (128, 1), BF16)
    b.sbuf("ksblk", (128, 128), BF16)
    b.sbuf("fout", (128, 1024), F32)
    b.sbuf("basesrow", (1, 576), F32)
    b.sbuf("wgs", (128, 64), BF16)
    b.sbuf("ccsb", (16, 8), F32)

    b.ps = [nc.alloc_psum_tensor(f"ps{i}", [128, 512], F32) for i in range(8)]


def preload(b):
    p, g, sb = b.p, b.g, b.sb
    toks = []
    t_ones = p.dop("gpsimd", lambda e: e.dma_start(out=sb["onesb"][:], in_=g["c_onesbf"]), label="ld.onesb", slot="pre.ones")
    b.st["t_onesb"] = (t_ones,)
    toks.append(t_ones)
    for n in ["texcl", "ones_col", "ones_row", "iota8", "capcol", "hsel", "msel"]:
        toks.append(p.dop("sync", (lambda n: lambda e: e.dma_start(out=sb[n][:], in_=g["c_" + n]))(n), label=f"ld.{n}", slot="pre"))
    for n in ["bq", "b1", "inpb", "dtb", "alog", "dpar", "convb", "convw"]:
        t_ = p.dop("sync", (lambda n: lambda e: e.dma_start(out=sb[n][:], in_=g["w_" + n]))(n), label=f"ld.{n}",
                   slot=("pre.alog" if n == "alog" else "pre"))
        if n == "alog":
            b.st["t_alog"] = (t_,)
        toks.append(t_)
    for n in ["ln1g", "ln1b", "ln2g", "ln2b"]:
        toks.append(p.dop("sync", (lambda n: lambda e: e.dma_start(out=sb[n][:], in_=g["w_" + n]))(n), label=f"ld.{n}", slot="pre"))
    for n, src in [("bkt", "w_bk"), ("bvt", "w_bv"), ("bot", "w_bo"), ("b2t", "w_b2"), ("moutbt", "w_moutb")]:
        toks.append(p.dop("sync", (lambda n, src: lambda e: e.dma_start(out=sb[n][:], in_=g[src]))(n, src), label=f"ld.{n}", slot="pre"))
    r = p.op("scalar", lambda e: e.activation(out=sb["acol"][:], in_=sb["alog"][:], func=AF.Exp), waits=b.st["t_alog"], label="acol")
    toks.append(p.tok(r))
    rz = p.op("vector", lambda e: e.memset(sb["stream"][:, 0:D], 0.0), label="zero_n")
    zt = p.tok(rz)
    rz2 = p.op("vector", lambda e: e.memset(sb["route"][:], 0.0), label="routez")
    toks.append(p.tok(rz2))
    rz3 = p.op("vector", lambda e: e.memset(sb["basesrow"][:], 0.0), label="basesz")
    toks.append(p.tok(rz3))
    re1 = p.op("vector", lambda e: e.memset(sb["epsc"][:, 0:1], LNEPS_OUT), label="eps5")
    re2 = p.op("vector", lambda e: e.memset(sb["epsc"][:, 1:2], LNEPS_IN), label="eps6")
    toks += [p.tok(re1), p.tok(re2)]
    for i in range((NSLOT + 128) // 128):
        toks.append(p.dop("scalar", (lambda i: lambda e: e.dma_start(
            out=g["XBUF"][i * 128:(i + 1) * 128, :], in_=sb["stream"][:, 0:D]))(i), waits=(zt,), label=f"xbz{i}", slot="pre"))
    toks.append(p.dop("scalar", lambda e: e.dma_start(
        out=g["OBUF"][NSLOT:NSLOT + 128, :], in_=sb["stream"][:, 0:D]), waits=(zt,), label="obz", slot="pre"))
    b.st["pre"] = toks


def ln_cols(b, x_ap, eps, mcol, rcol, waits, scr_base, tag):
    p, sb = b.p, b.sb
    bn = sb["small"]
    s = scr_base
    r1 = p.op("vector", lambda e: e.bn_stats(out=bn[:, s:s + 6], in_=x_ap[:, 0:512]), waits=waits, label=f"{tag}.bn0")
    r2 = p.op("vector", lambda e: e.bn_stats(out=bn[:, s + 6:s + 12], in_=x_ap[:, 512:1024]), waits=waits, label=f"{tag}.bn1")
    r3 = p.op("vector", lambda e: e.bn_aggr(out=bn[:, s + 12:s + 14], in_=bn[:, s:s + 12]), label=f"{tag}.ag")
    r4 = p.op("vector", lambda e: e.tensor_copy(out=mcol, in_=bn[:, s + 12:s + 13]), label=f"{tag}.m")
    r5 = p.op("scalar", lambda e: e.activation(out=bn[:, s + 14:s + 15], in_=bn[:, s + 13:s + 14], func=AF.Sqrt, bias=eps),
              waits=(p.tok(r3),), label=f"{tag}.sq")
    r6 = p.op("vector", lambda e: e.reciprocal(out=rcol, in_=bn[:, s + 14:s + 15]), waits=(p.tok(r5),), label=f"{tag}.rc")
    return [p.tok(r4), p.tok(r6)], [p.tok(r5), p.tok(r6)]


def build_ph0(b):
    p, g, sb = b.p, b.g, b.sb
    stream = sb["stream"]
    stat = sb["stat"]
    scr_free = {0: (), 1: ()}
    for c in range(NCHUNK):
        par = c % 2
        buf = stream[:, par * 3072: par * 3072 + 3072]
        xb, lno, lna = buf[:, 0:D], buf[:, D:2 * D], buf[:, 2 * D:3 * D]
        tld = p.dop("gpsimd", (lambda xb, c: lambda e: e.dma_start(out=xb, in_=g["xs"][c * 128:(c + 1) * 128, :]))(xb, c),
                    waits=b.st.get(f"ph0.free{par}", ()) + tuple(b.st["pre"] if c == 0 else ()), label=f"ph0.ld{c}", slot=f"ph0.ld{par}")
        mcol, rcol = stat[:, 0 + par:1 + par], stat[:, 2 + par:3 + par]
        stoks, sfree = ln_cols(b, xb, b.sb["epsc"][:, 0:1], mcol, rcol, (tld,) + scr_free[par], 16 + par * 16, f"ph0.s{c}")
        r = p.op("vector", (lambda lno, xb, mcol, rcol: lambda e: e.tensor_scalar(
            out=lno, in0=xb, scalar1=mcol, scalar2=rcol, op0=ALU.subtract, op1=ALU.mult))(lno, xb, mcol, rcol),
            waits=stoks, label=f"ph0.lno{c}")
        tlno = p.tok(r)
        m1, r1c = stat[:, 4 + par:5 + par], stat[:, 6 + par:7 + par]
        stoks1, sfree1 = ln_cols(b, lno, b.sb["epsc"][:, 1:2], m1, r1c, (tlno,), 48 + par * 16, f"ph0.t{c}")
        scr_free[par] = tuple(sfree) + tuple(sfree1)
        r = p.op("vector", (lambda lna, lno, m1, r1c: lambda e: e.tensor_scalar(
            out=lna, in0=lno, scalar1=m1, scalar2=r1c, op0=ALU.subtract, op1=ALU.mult))(lna, lno, m1, r1c),
            waits=stoks1, label=f"ph0.n1{c}")
        r = p.op("vector", (lambda lna: lambda e: e.tensor_tensor(out=lna, in0=lna, in1=sb["ln1g"][:], op=ALU.mult))(lna),
                 label=f"ph0.g{c}")
        r = p.op("vector", (lambda lna: lambda e: e.tensor_tensor(out=lna, in0=lna, in1=sb["ln1b"][:], op=ALU.add))(lna),
                 label=f"ph0.b{c}")
        tlna = p.tok(r)
        t1 = p.dop("scalar", (lambda xb, c: lambda e: e.dma_start(out=g["RES"][c * 128:(c + 1) * 128, :], in_=xb))(xb, c),
                   waits=(tlno,), label=f"ph0.sr{c}", slot=f"ph0.st{par}")
        t2 = p.dop("scalar", (lambda lno, c: lambda e: e.dma_start(out=g["LNO"][c * 128:(c + 1) * 128, :], in_=lno))(lno, c),
                   waits=(tlna,), label=f"ph0.so{c}", slot=f"ph0.st{par}")
        t3 = p.dop("scalar", (lambda lna, c: lambda e: e.dma_start(out=g["LNX"][c * 128:(c + 1) * 128, :], in_=lna))(lna, c),
                   waits=(tlna,), label=f"ph0.sx{c}", slot=f"ph0.st{par}")
        b.st[f"ph0.free{par}"] = (t1, t2, t3)



def build_attn(b):
    ksub = int(os.environ.get("KSUB", "99"))
    p, g, sb, ps = b.p, b.g, b.sb, b.ps
    aw0, aw1, am, stream = sb["aw0"], sb["aw1"], sb["am"], sb["stream"]
    lnaT = aw0[:, 0:16384].rearrange("p (m t) -> p m t", m=8)
    phiK = aw0[:, 16384:32768]
    V = aw1[:, 0:16384]
    phiQ = aw1[:, 16384:32768].rearrange("p (m t) -> p m t", m=8)
    attnT = lnaT
    bk = Banks(ps)

    for bi in range(NB):
        T0 = bi * SEQ
        CH0 = T0 // 128
        a1 = []
        for c in range(16):
            t = p.dop("sync", (lambda c, T0=T0: lambda e: e.dma_start_transpose(
                out=lnaT[:, :, c * 128:(c + 1) * 128],
                in_=g["LNX"][T0 + c * 128:T0 + (c + 1) * 128, :]))(c),
                waits=b.st.get("at.lnafree", ()), label=f"at.a1.{bi}.{c}", slot="at.a1")
            a1.append(t)
        if ksub < 2:
            continue
        qdone = []
        wqtok = [None, None]
        for hm in range(8):
            lastmm = None
            for d in range(8):
                sl = (hm * 8 + d) % 2
                wt = p.dop("gpsimd", (lambda sl, d, hm: lambda e: e.dma_start(
                    out=sb["wqslot"][:, sl * 128:(sl + 1) * 128],
                    in_=g["wq"][d * 128:(d + 1) * 128, hm * 128:(hm + 1) * 128]))(sl, d, hm),
                    waits=(wqtok[sl],), label=f"at.wq.{bi}.{hm}.{d}", slot=f"at.wq{sl}")
                for ch in range(4):
                    bank, bfree = bk.get(ch)
                    lastmm = p.op("tensor", (lambda sl, d, ch, bank: lambda e: e.matmul(
                        out=bank[:, 0:512], lhsT=sb["wqslot"][:, sl * 128:(sl + 1) * 128],
                        rhs=lnaT[:, d, ch * 512:(ch + 1) * 512],
                        start=(d == 0), stop=(d == 7)))(sl, d, ch, bank),
                        waits=(wt, a1, (bfree if d == 0 else ()),
                               (b.st.get("at.phiqfree", ()) if hm == 0 and d == 0 else ())),
                        label=f"at.q.{bi}.{hm}.{d}.{ch}")
                wqtok[sl] = p.tok(lastmm)
            tq = p.tok(lastmm)
            for ch in range(4):
                bank, _ = bk.get(ch)
                r = p.op("vector", (lambda ch, hm, bank: lambda e: e.tensor_scalar(
                    out=phiQ[:, hm, ch * 512:(ch + 1) * 512], in0=bank[:, 0:512],
                    scalar1=sb["bq"][:, hm:hm + 1], scalar2=0.0, op0=ALU.add, op1=ALU.max))(ch, hm, bank),
                    waits=(tq,), label=f"at.qp.{bi}.{hm}.{ch}")
                r = p.op("vector", (lambda ch, hm: lambda e: e.tensor_scalar(
                    out=phiQ[:, hm, ch * 512:(ch + 1) * 512], in0=phiQ[:, hm, ch * 512:(ch + 1) * 512],
                    scalar1=KEPS, scalar2=None, op0=ALU.add))(ch, hm), label=f"at.qe.{bi}.{hm}.{ch}")
                bk.done(ch, p.tok(r))
                qdone.append(p.tok(r))
        if ksub < 3:
            continue
        bandtoks = []
        for d in range(8):
            t1 = p.dop("gpsimd", (lambda d: lambda e: e.dma_start(
                out=am[:, d * 1024:(d + 1) * 1024], in_=g["wk"][d * 128:(d + 1) * 128, :]))(d),
                waits=b.st.get("at.amfree", ()), label=f"at.wk.{bi}.{d}", slot="at.band")
            t2 = p.dop("gpsimd", (lambda d: lambda e: e.dma_start(
                out=am[:, 8192 + d * 1024:8192 + (d + 1) * 1024], in_=g["wv"][d * 128:(d + 1) * 128, :]))(d),
                waits=b.st.get("at.amfree", ()), label=f"at.wv.{bi}.{d}", slot="at.band")
            bandtoks += [t1, t2]
        kvdone = []
        for c in range(16):
            for half in range(2):
                ik = 4 + (c * 2 + half) % 2
                iv = 6 + (c * 2 + half) % 2
                bK, fK = bk.get(ik)
                bV, fV = bk.get(iv)
                lastK = lastV = None
                for d in range(8):
                    lastK = p.op("tensor", (lambda c, half, d, bK: lambda e: e.matmul(
                        out=bK[:, 0:512], lhsT=lnaT[:, d, c * 128:(c + 1) * 128],
                        rhs=am[:, d * 1024 + half * 512: d * 1024 + half * 512 + 512],
                        start=(d == 0), stop=(d == 7)))(c, half, d, bK),
                        waits=(bandtoks, a1, (fK if d == 0 else ()), (b.st.get("at.phikfree", ()) if c == 0 and half == 0 and d == 0 else ())),
                        label=f"at.k.{bi}.{c}.{half}.{d}")
                for d in range(8):
                    lastV = p.op("tensor", (lambda c, half, d, bV: lambda e: e.matmul(
                        out=bV[:, 0:512], lhsT=lnaT[:, d, c * 128:(c + 1) * 128],
                        rhs=am[:, 8192 + d * 1024 + half * 512: 8192 + d * 1024 + half * 512 + 512],
                        start=(d == 0), stop=(d == 7)))(c, half, d, bV),
                        waits=((fV if d == 0 else ()),), label=f"at.v.{bi}.{c}.{half}.{d}")
                tK, tV = p.tok(lastK), p.tok(lastV)
                sl = slice(c * 1024 + half * 512, c * 1024 + half * 512 + 512)
                r1 = p.op("vector", (lambda sl, bK: lambda e: e.tensor_tensor(
                    out=phiK[:, sl], in0=bK[:, 0:512], in1=sb["bkt"][:, (sl.start % 1024):(sl.start % 1024) + 512], op=ALU.add))(sl, bK),
                    waits=(tK,), label=f"at.kp1.{bi}.{c}.{half}")
                r1 = p.op("vector", (lambda sl: lambda e: e.tensor_scalar(
                    out=phiK[:, sl], in0=phiK[:, sl],
                    scalar1=0.0, scalar2=KEPS, op0=ALU.max, op1=ALU.add))(sl), label=f"at.kp2.{bi}.{c}.{half}")
                r2 = p.op("vector", (lambda sl, bV: lambda e: e.tensor_tensor(
                    out=V[:, sl], in0=bV[:, 0:512], in1=sb["bvt"][:, (sl.start % 1024):(sl.start % 1024) + 512], op=ALU.add))(sl, bV),
                    waits=(tV,), label=f"at.vp.{bi}.{c}.{half}")
                bk.done(ik, p.tok(r1))
                bk.done(iv, p.tok(r2))
                kvdone += [p.tok(r1), p.tok(r2)]
        if ksub < 4:
            continue
        if bi == 0:
            rz = p.op("vector", lambda e: e.memset(sb["ksblk"][:], 0.0), label="at.ksz")
            kvdone.append(p.tok(rz))
            rz = p.op("vector", lambda e: e.memset(sb["kvs"][:], 0.0), label="at.kvz")
            kvdone.append(p.tok(rz))
        kvtoks = []
        for j in range(8):
            for h2 in range(2):
                h = 2 * j + h2
                ib = h % 2
                bank, bfree = bk.get(ib)
                last = None
                for stile in range(16):
                    last = p.op("tensor", (lambda h, stile, bank: lambda e: e.matmul(
                        out=bank[0:64, 0:64], lhsT=phiK[:, stile * 1024 + h * 64: stile * 1024 + h * 64 + 64],
                        rhs=V[:, stile * 1024 + h * 64: stile * 1024 + h * 64 + 64],
                        start=(stile == 0), stop=False))(h, stile, bank),
                        waits=(kvdone, (bfree if stile == 0 else ()),
                               (b.st.get("at.kvsfree", ()) if j == 0 and h2 == 0 and stile == 0 else ())),
                        label=f"at.kv.{bi}.{h}.{stile}")
                for stile in range(16):
                    last = p.op("tensor", (lambda h, stile, bank: lambda e: e.matmul(
                        out=bank[0:64, 64:65], lhsT=phiK[:, stile * 1024 + h * 64: stile * 1024 + h * 64 + 64],
                        rhs=sb["onesb"][:, 0:1], start=False, stop=(stile == 15)))(h, stile, bank),
                        waits=(b.st["t_onesb"],), label=f"at.ks.{bi}.{h}.{stile}")
                tkv = p.tok(last)
                pb = h2 * 64
                r1 = p.op("vector", (lambda j, pb, bank: lambda e: e.tensor_copy(
                    out=sb["kvs"][pb:pb + 64, j * 128 + pb:j * 128 + pb + 64], in_=bank[0:64, 0:64]))(j, pb, bank),
                    waits=(tkv,), label=f"at.kvc.{bi}.{h}")
                r2 = p.op("vector", (lambda j, h, pb, bank: lambda e: e.tensor_copy(
                    out=sb["ksblk"][pb:pb + 64, j * 16 + h: j * 16 + h + 1], in_=bank[0:64, 64:65]))(j, h, pb, bank),
                    label=f"at.ksc.{bi}.{h}")
                bk.done(ib, p.tok(r1), p.tok(r2))
                kvtoks += [p.tok(r1), p.tok(r2)]
        if ksub < 5:
            continue
        attnst = []
        for c in range(16):
            par = c % 2
            bD, fD = bk.get(2)
            lastD = None
            for t in range(8):
                lastD = p.op("tensor", (lambda c, t, bD: lambda e: e.matmul(
                    out=bD[:, 0:16], lhsT=phiQ[:, t, c * 128:(c + 1) * 128],
                    rhs=sb["ksblk"][:, t * 16:(t + 1) * 16],
                    start=(t == 0), stop=(t == 7)))(c, t, bD),
                    waits=(kvtoks, qdone, (fD if t == 0 else ())),
                    label=f"at.den.{bi}.{c}.{t}")
            rD = p.op("vector", (lambda bD: lambda e: e.reciprocal(
                out=sb["small"][:, 490:506], in_=bD[:, 0:16]))(bD),
                waits=(p.tok(lastD), b.st.get("at.rdfree", ())), label=f"at.rd.{bi}.{c}")
            tRD = p.tok(rD)
            bk.done(2, tRD)
            lastN = [None, None]
            for j in range(8):
                ib = 3 + j // 4
                bank, bfree = bk.get(ib)
                r = p.op("tensor", (lambda c, j, bank: lambda e: e.matmul(
                    out=bank[:, (j % 4) * 128:(j % 4) * 128 + 128],
                    lhsT=phiQ[:, j, c * 128:(c + 1) * 128],
                    rhs=sb["kvs"][:, j * 128:(j + 1) * 128],
                    start=(j % 4 == 0), stop=(j % 4 == 3)))(c, j, bank),
                    waits=((bfree if j % 4 == 0 else ()),), label=f"at.num.{bi}.{c}.{j}")
                lastN[j // 4] = r
            tN = [p.tok(lastN[0]), p.tok(lastN[1])]
            achunk = stream[:, par * 1024:par * 1024 + 1024]
            sctoks = []
            for h in range(16):
                j = h // 2
                bank, _ = bk.get(3 + j // 4)
                r = p.op("vector", (lambda h, j, achunk, bank: lambda e: e.tensor_scalar(
                    out=achunk[:, h * 64:(h + 1) * 64], in0=bank[:, (j % 4) * 128 + (h % 2) * 64:(j % 4) * 128 + (h % 2) * 64 + 64],
                    scalar1=sb["small"][:, 490 + h:491 + h], scalar2=None, op0=ALU.mult))(h, j, achunk, bank),
                    waits=(tN[j // 4], tRD, (b.st.get(f"at.acfree{par}", ()) if h == 0 else ())),
                    label=f"at.as.{bi}.{c}.{h}")
                sctoks.append(p.tok(r))
            b.st["at.rdfree"] = (sctoks[-1],)
            bk.done(3, sctoks[-1])
            bk.done(4, sctoks[-1])
            tst = p.dop("scalar", (lambda c, achunk, T0=T0: lambda e: e.dma_start(
                out=g["ATTN"][T0 + c * 128:T0 + (c + 1) * 128, :], in_=achunk))(c, achunk),
                waits=(sctoks[-1],), label=f"at.ast.{bi}.{c}", slot=f"at.ast{par}")
            b.st[f"at.acfree{par}"] = (tst,)
            attnst.append(tst)
        if ksub < 6:
            continue
        a5t = []
        for c in range(16):
            t = p.dop("sync", (lambda c, T0=T0: lambda e: e.dma_start_transpose(
                out=attnT[:, :, c * 128:(c + 1) * 128],
                in_=g["ATTN"][T0 + c * 128:T0 + (c + 1) * 128, :]))(c),
                waits=(attnst[c], kvdone[-1], qdone[-1]), label=f"at.a5t.{bi}.{c}", slot="at.a5t")
            a5t.append(t)
        wotoks = []
        for t in range(8):
            tt_ = p.dop("gpsimd", (lambda t: lambda e: e.dma_start(
                out=am[:, t * 1024:(t + 1) * 1024], in_=g["wo"][t * 128:(t + 1) * 128, :]))(t),
                waits=(kvtoks[-1],), label=f"at.wo.{bi}.{t}", slot="at.wo")
            wotoks.append(tt_)
        scr_free = {0: (), 1: ()}
        lastomm = None
        for c in range(16):
            par = c % 2
            lnobuf = stream[:, 2048 + par * 1024: 2048 + par * 1024 + 1024]
            xres = stream[:, 4096 + par * 1024: 4096 + par * 1024 + 1024]
            tlno = p.dop("sync", (lambda c, lnobuf, T0=T0: lambda e: e.dma_start(
                out=lnobuf, in_=g["LNO"][T0 + c * 128:T0 + (c + 1) * 128, :]))(c, lnobuf),
                waits=b.st.get(f"at.lnofree{par}", ()), label=f"at.lno.{bi}.{c}", slot=f"at.lno{par}")
            htoks = []
            for half in range(2):
                io = 5 + half
                bO, fO = bk.get(io)
                last = None
                for t in range(8):
                    last = p.op("tensor", (lambda c, half, t, bO: lambda e: e.matmul(
                        out=bO[:, 0:512], lhsT=attnT[:, t, c * 128:(c + 1) * 128],
                        rhs=am[:, t * 1024 + half * 512: t * 1024 + half * 512 + 512],
                        start=(t == 0), stop=(t == 7)))(c, half, t, bO),
                        waits=(wotoks, a5t, (fO if t == 0 else ())),
                        label=f"at.o.{bi}.{c}.{half}.{t}")
                lastomm = last
                tO = p.tok(last)
                r = p.op("vector", (lambda half, xres, bO: lambda e: e.tensor_tensor(
                    out=xres[:, half * 512:(half + 1) * 512], in0=bO[:, 0:512],
                    in1=sb["bot"][:, half * 512:(half + 1) * 512], op=ALU.add))(half, xres, bO),
                    waits=(tO, tlno, b.st.get(f"at.xrfree{par}", ())), label=f"at.ob.{bi}.{c}.{half}")
                r = p.op("vector", (lambda half, xres, lnobuf: lambda e: e.tensor_tensor(
                    out=xres[:, half * 512:(half + 1) * 512], in0=xres[:, half * 512:(half + 1) * 512],
                    in1=lnobuf[:, half * 512:(half + 1) * 512], op=ALU.add))(half, xres, lnobuf),
                    label=f"at.or.{bi}.{c}.{half}")
                bk.done(io, p.tok(r))
                htoks.append(p.tok(r))
            mcol = sb["stat"][:, 8 + par:9 + par]
            rcol = sb["stat"][:, 10 + par:11 + par]
            stoks, sfree = ln_cols(b, xres, sb["epsc"][:, 1:2], mcol, rcol,
                                   (htoks, scr_free[par]), 80 + par * 16, f"at.l2.{bi}.{c}")
            scr_free[par] = tuple(sfree)
            ln2x = stream[:, par * 1024:par * 1024 + 1024]
            r = p.op("vector", (lambda ln2x, xres, mcol, rcol: lambda e: e.tensor_scalar(
                out=ln2x, in0=xres, scalar1=mcol, scalar2=rcol, op0=ALU.subtract, op1=ALU.mult))(ln2x, xres, mcol, rcol),
                waits=(stoks, b.st.get(f"at.acfree{par}", ())), label=f"at.l2a.{bi}.{c}")
            r = p.op("vector", (lambda ln2x: lambda e: e.tensor_tensor(out=ln2x, in0=ln2x, in1=sb["ln2g"][:], op=ALU.mult))(ln2x),
                     label=f"at.l2g.{bi}.{c}")
            r = p.op("vector", (lambda ln2x: lambda e: e.tensor_tensor(out=ln2x, in0=ln2x, in1=sb["ln2b"][:], op=ALU.add))(ln2x),
                     label=f"at.l2b.{bi}.{c}")
            tl2 = p.tok(r)
            t1 = p.dop("scalar", (lambda c, xres, T0=T0: lambda e: e.dma_start(
                out=g["XRES"][T0 + c * 128:T0 + (c + 1) * 128, :], in_=xres))(c, xres),
                waits=(tl2,), label=f"at.sx.{bi}.{c}", slot=f"at.sx{par}")
            t2 = p.dop("scalar", (lambda c, ln2x, T0=T0: lambda e: e.dma_start(
                out=g["LNX"][T0 + c * 128:T0 + (c + 1) * 128, :], in_=ln2x))(c, ln2x),
                waits=(tl2,), label=f"at.sl.{bi}.{c}", slot=f"at.sl{par}")
            b.st[f"at.lnofree{par}"] = (p.tok(htoks[-1].rec),)
            b.st[f"at.xrfree{par}"] = (t1,)
            b.st[f"at.acfree{par}"] = b.st.get(f"at.acfree{par}", ()) + (t2,)
        tol = p.tok(lastomm)
        b.st["at.lnafree"] = (tol,)
        b.st["at.amfree"] = (tol,)
        b.st["at.phikfree"] = (kvtoks[-1],)
        b.st["at.phiqfree"] = (attnst[-1],)
        b.st["at.kvsfree"] = (attnst[-1],)


def build_ffn(b):
    p, g, sb, ps = b.p, b.g, b.sb, b.ps
    aw0, aw1, am, stream = sb["aw0"], sb["aw1"], sb["am"], sb["stream"]
    bk = Banks(ps)
    wtoks = []
    for d in range(8):
        t = p.dop("gpsimd", (lambda d: lambda e: e.dma_start(
            out=aw0[:, d * 4096:(d + 1) * 4096], in_=g["w1"][d * 128:(d + 1) * 128, :]))(d),
            label=f"ff.w1.{d}", slot="ff.w")
        wtoks.append(t)
    for f in range(8):
        t = p.dop("gpsimd", (lambda f: lambda e: e.dma_start(
            out=aw1[:, f * 4096:(f + 1) * 4096].rearrange("p (q n) -> p q n", q=4),
            in_=g["w2"][f * 512:(f + 1) * 512, :].rearrange("(q r) n -> r q n", q=4)))(f),
            label=f"ff.w2.{f}", slot="ff.w")
        wtoks.append(t)
    lx = am[:, 12288:16384].rearrange("p (m t) -> p m t", m=8)
    chunks = []
    t0 = 0
    while t0 < NTOK:
        cn = min(384, NTOK - t0)
        chunks.append((t0, cn))
        t0 += cn
    scr_free = {0: (), 1: ()}
    gel_free = {}
    for ci, (t0, cn) in enumerate(chunks):
        tlx = p.dop("sync", (lambda t0, cn: lambda e: e.dma_start_transpose(
            out=lx[:, :, 0:cn], in_=g["LNX"][t0:t0 + cn, :]))(t0, cn),
            waits=b.st.get("ff.lxfree", ()), label=f"ff.lx.{ci}", slot="ff.lx")
        geltoks = []
        for f in range(32):
            ib = f % 2
            bank, bfree = bk.get(ib)
            last = None
            for d in range(8):
                last = p.op("tensor", (lambda f, d, cn, bank: lambda e: e.matmul(
                    out=bank[:, 0:cn], lhsT=aw0[:, d * 4096 + f * 128: d * 4096 + (f + 1) * 128],
                    rhs=lx[:, d, 0:cn], start=(d == 0), stop=(d == 7)))(f, d, cn, bank),
                    waits=(wtoks, tlx, (bfree if d == 0 else ())),
                    label=f"ff.h.{ci}.{f}.{d}")
            th = p.tok(last)
            r = p.op("scalar", (lambda f, cn, bank: lambda e: e.activation(
                out=am[:, f * 384:f * 384 + cn], in_=bank[:, 0:cn], func=AF.Gelu_apprx_tanh,
                bias=sb["b1"][:, f:f + 1]))(f, cn, bank),
                waits=(th, gel_free.get(f, ())), label=f"ff.g.{ci}.{f}")
            tg = p.tok(r)
            bk.done(ib, tg)
            geltoks.append(tg)
        nsub = (cn + 127) // 128
        for sct in range(nsub):
            sn = min(128, cn - sct * 128)
            par = (ci * 3 + sct) % 2
            xrb = stream[:, par * 1024:par * 1024 + 1024]
            rrb = stream[:, 2048 + par * 1024:2048 + par * 1024 + 1024]
            xnew = stream[:, 4096 + par * 1024:4096 + par * 1024 + 1024]
            tr1 = p.dop("sync", (lambda t0, sct, sn, xrb: lambda e: e.dma_start(
                out=xrb[0:sn, :], in_=g["XRES"][t0 + sct * 128: t0 + sct * 128 + sn, :]))(t0, sct, sn, xrb),
                waits=b.st.get(f"ff.pfree{par}", ()), label=f"ff.xr.{ci}.{sct}", slot=f"ff.ld{par}")
            tr2 = p.dop("sync", (lambda t0, sct, sn, rrb: lambda e: e.dma_start(
                out=rrb[0:sn, :], in_=g["RES"][t0 + sct * 128: t0 + sct * 128 + sn, :]))(t0, sct, sn, rrb),
                waits=b.st.get(f"ff.pfree{par}", ()), label=f"ff.rr.{ci}.{sct}", slot=f"ff.ld{par}")
            etoks = []
            for half in range(2):
                ib = 2 + (sct * 2 + half) % 4
                bank, bfree = bk.get(ib)
                last = None
                for f in range(32):
                    last = p.op("tensor", (lambda f, sct, sn, half, bank: lambda e: e.matmul(
                        out=bank[0:sn, 0:512], lhsT=am[:, f * 384 + sct * 128: f * 384 + sct * 128 + sn],
                        rhs=aw1[:, (f // 4) * 4096 + (f % 4) * 1024 + half * 512: (f // 4) * 4096 + (f % 4) * 1024 + half * 512 + 512],
                        start=(f == 0), stop=(f == 31)))(f, sct, sn, half, bank),
                        waits=(geltoks, (bfree if f == 0 else ())),
                        label=f"ff.o.{ci}.{sct}.{half}.{f}")
                gel_free[0] = gel_free.get(0, ()) + ()
                tmm = p.tok(last)
                r = p.op("vector", (lambda sn, half, xnew, bank: lambda e: e.tensor_tensor(
                    out=xnew[0:sn, half * 512:(half + 1) * 512], in0=bank[0:sn, 0:512],
                    in1=sb["b2t"][0:sn, half * 512:(half + 1) * 512], op=ALU.add))(sn, half, xnew, bank),
                    waits=(tmm, tr1, tr2), label=f"ff.e1.{ci}.{sct}.{half}")
                r = p.op("vector", (lambda sn, half, xnew, xrb: lambda e: e.tensor_tensor(
                    out=xnew[0:sn, half * 512:(half + 1) * 512], in0=xnew[0:sn, half * 512:(half + 1) * 512],
                    in1=xrb[0:sn, half * 512:(half + 1) * 512], op=ALU.add))(sn, half, xnew, xrb),
                    label=f"ff.e2.{ci}.{sct}.{half}")
                r = p.op("vector", (lambda sn, half, xnew, rrb: lambda e: e.tensor_tensor(
                    out=xnew[0:sn, half * 512:(half + 1) * 512], in0=xnew[0:sn, half * 512:(half + 1) * 512],
                    in1=rrb[0:sn, half * 512:(half + 1) * 512], op=ALU.add))(sn, half, xnew, rrb),
                    label=f"ff.e3.{ci}.{sct}.{half}")
                te = p.tok(r)
                bk.done(ib, te)
                etoks.append(te)
            gidx = t0 // 128 + sct
            tokidx = t0 + sct * 128
            mcol = sb["stat"][:, 12 + par:13 + par]
            rcol = sb["stat"][:, 14 + par:15 + par]
            stoks, sfree = ln_cols(b, xnew, sb["epsc"][:, 0:1], mcol, rcol,
                                   (etoks, scr_free[par]), 80 + par * 16, f"ff.l.{ci}.{sct}")
            scr_free[par] = tuple(sfree)
            lnx = xrb
            r = p.op("vector", (lambda lnx, xnew, mcol, rcol, sn: lambda e: e.tensor_scalar(
                out=lnx[0:sn, :], in0=xnew[0:sn, :], scalar1=mcol, scalar2=rcol,
                op0=ALU.subtract, op1=ALU.mult))(lnx, xnew, mcol, rcol, sn),
                waits=(stoks,), label=f"ff.lna.{ci}.{sct}")
            tl = p.tok(r)
            t1 = p.dop("scalar", (lambda tokidx, sn, xnew: lambda e: e.dma_start(
                out=g["RES"][tokidx:tokidx + sn, :], in_=xnew))(tokidx, sn, xnew[0:sn, :]),
                waits=(tl,), label=f"ff.sr.{ci}.{sct}", slot=f"ff.st{par}")
            t2 = p.dop("scalar", (lambda tokidx, sn, lnx: lambda e: e.dma_start(
                out=g["LNX"][tokidx:tokidx + sn, :], in_=lnx))(tokidx, sn, lnx[0:sn, :]),
                waits=(tl,), label=f"ff.sl.{ci}.{sct}", slot=f"ff.st{par}")
            b.st[f"ff.pfree{par}"] = (t1, t2)
        lastsub = etoks[-1]
        for f in range(32):
            gel_free[f] = (lastsub,)
        b.st["ff.lxfree"] = (p.tok(last),)




def build_moe(b, call):
    p, g, sb, ps = b.p, b.g, b.sb, b.ps
    aw0, aw1, am, stream = sb["aw0"], sb["aw1"], sb["am"], sb["stream"]
    bk = Banks(ps)
    route, ridx, small, stat = sb["route"], sb["ridx"], sb["small"], sb["stat"]
    CC = f"{call}"
    twg = p.dop("gpsimd", lambda e: e.dma_start(
        out=sb["wgs"][:].rearrange("p (t e) -> p t e", e=8),
        in_=g["wg"].rearrange("(t p) e -> p t e", p=128)), label=f"mo.wg.{call}", slot=f"mo.wg{CC}")

    lxT = am[:, 12288:13312].rearrange("p (m t) -> p m t", m=8)
    snaptok = None
    cnt_mm_tok = None
    for c in range(NCHUNK):
        tlx = p.dop("sync", (lambda c: lambda e: e.dma_start_transpose(
            out=lxT[:, :, :], in_=g["LNX"][c * 128:(c + 1) * 128, :]))(c),
            waits=b.st.get("mo.lxfree", ()), label=f"mo.lx.{call}.{c}", slot=f"mo.lx{CC}")
        bL, fL = bk.get(2)
        last = None
        for d in range(8):
            last = p.op("tensor", (lambda d, bL: lambda e: e.matmul(
                out=bL[:, 0:8], lhsT=lxT[:, d, :], rhs=sb["wgs"][:, d * 8:(d + 1) * 8],
                start=(d == 0), stop=(d == 7)))(d, bL),
                waits=(twg, tlx, (fL if d == 0 else ())), label=f"mo.lg.{call}.{c}.{d}")
        tl = p.tok(last)
        lg = small[:, 128:136]
        r = p.op("vector", (lambda bL, lg: lambda e: e.tensor_copy(out=lg, in_=bL[:, 0:8]))(bL, lg),
                 waits=(tl, b.st.get("mo.routescr", ())), label=f"mo.cp.{call}.{c}")
        b.st["mo.lxfree"] = (p.tok(r),)
        bk.done(2, p.tok(r))
        mx = small[:, 136:144]
        r = p.op("vector", (lambda lg, mx: lambda e: e.max(out=mx, in_=lg))(lg, mx), label=f"mo.mx.{call}.{c}")
        ngt = small[:, 144:145]
        r = p.op("vector", (lambda mx, ngt: lambda e: e.tensor_scalar(
            out=ngt, in0=mx[:, 0:1], scalar1=-1.0, scalar2=None, op0=ALU.mult))(mx, ngt), label=f"mo.ng.{call}.{c}")
        tng = p.tok(r)
        ev = small[:, 152:160]
        r5 = p.op("scalar", (lambda lg, ev, ngt: lambda e: e.activation(
            out=ev, in_=lg, func=AF.Exp, bias=ngt))(lg, ev, ngt), waits=(tng,), label=f"mo.ex.{call}.{c}")
        ev2 = small[:, 160:162]
        r6 = p.op("scalar", (lambda mx, ev2, ngt: lambda e: e.activation(
            out=ev2, in_=mx[:, 0:2], func=AF.Exp, bias=ngt))(mx, ev2, ngt), label=f"mo.ex2.{call}.{c}")
        ssum = small[:, 162:163]
        r = p.op("vector", (lambda ev, ssum: lambda e: e.reduce_sum(
            out=ssum, in_=ev, axis=mybir.AxisListType.X))(ev, ssum), waits=(p.tok(r5),), label=f"mo.sm.{call}.{c}")
        rs = small[:, 163:164]
        r = p.op("vector", (lambda ssum, rs: lambda e: e.reciprocal(out=rs, in_=ssum))(ssum, rs), label=f"mo.rs.{call}.{c}")
        graw = route[:, c * 10 + 2:c * 10 + 4]
        r = p.op("vector", (lambda ev2, rs, graw: lambda e: e.tensor_scalar(
            out=graw, in0=ev2, scalar1=rs, scalar2=None, op0=ALU.mult))(ev2, rs, graw),
            waits=(p.tok(r6),), label=f"mo.gr.{call}.{c}")
        oh0 = small[:, 164:172]
        oh1 = small[:, 172:180]
        r = p.op("vector", (lambda lg, mx, oh0: lambda e: e.tensor_scalar(
            out=oh0, in0=lg, scalar1=mx[:, 0:1], scalar2=None, op0=ALU.is_equal))(lg, mx, oh0), label=f"mo.o0.{call}.{c}")
        r = p.op("vector", (lambda lg, mx, oh1: lambda e: e.tensor_scalar(
            out=oh1, in0=lg, scalar1=mx[:, 1:2], scalar2=None, op0=ALU.is_equal))(lg, mx, oh1), label=f"mo.o1.{call}.{c}")
        e0 = route[:, c * 10 + 0:c * 10 + 1]
        e1 = route[:, c * 10 + 1:c * 10 + 2]
        tmp8 = small[:, 180:188]
        r = p.op("vector", (lambda oh0, tmp8, e0: lambda e: e.scalar_tensor_tensor(
            out=tmp8, in0=oh0, scalar=1.0, in1=sb["iota8"][:], op0=ALU.mult, op1=ALU.mult,
            accum_out=e0))(oh0, tmp8, e0), label=f"mo.e0.{call}.{c}")
        r = p.op("vector", (lambda oh1, tmp8, e1: lambda e: e.scalar_tensor_tensor(
            out=tmp8, in0=oh1, scalar=1.0, in1=sb["iota8"][:], op0=ALU.mult, op1=ALU.mult,
            accum_out=e1))(oh1, tmp8, e1), label=f"mo.e1.{call}.{c}")
        toh = p.tok(r)
        bS, fS = bk.get(1)
        if c == 0:
            rz0 = p.op("vector", lambda e: e.memset(sb["basesrow"][0:1, 528:544], 0.0), label=f"mo.rz.{call}")
        rsnap = p.op("vector", (lambda c: lambda e: e.tensor_copy(
            out=sb["basesrow"][0:1, c * 16:(c + 1) * 16], in_=sb["basesrow"][0:1, 528:544]))(c),
            label=f"mo.snap.{call}.{c}")
        tsnap = p.tok(rsnap)
        bT, fT = bk.get(0)
        r1 = p.op("tensor", (lambda oh0, bT: lambda e: e.matmul(
            out=bT[:, 0:8], lhsT=sb["texcl"][:], rhs=oh0, start=True, stop=False))(oh0, bT),
            waits=(toh, (fT if True else ())), label=f"mo.t0.{call}.{c}")
        r2 = p.op("tensor", (lambda oh1, bT: lambda e: e.matmul(
            out=bT[:, 8:16], lhsT=sb["texcl"][:], rhs=oh1, start=False, stop=True))(oh1, bT),
            label=f"mo.t1.{call}.{c}")
        tT = p.tok(r2)
        bS2, fS2 = bk.get(1)
        r3 = p.op("tensor", (lambda oh0, bS2: lambda e: e.matmul(
            out=bS2[0:1, 0:8], lhsT=sb["ones_col"][:], rhs=oh0,
            start=True, stop=False))(oh0, bS2),
            waits=(toh, fS2), label=f"mo.c0.{call}.{c}")
        r4 = p.op("tensor", (lambda oh1, bS2: lambda e: e.matmul(
            out=bS2[0:1, 8:16], lhsT=sb["ones_col"][:], rhs=oh1,
            start=False, stop=True))(oh1, bS2),
            label=f"mo.c1.{call}.{c}")
        radd = p.op("vector", (lambda bS2: lambda e: e.tensor_tensor(
            out=sb["basesrow"][0:1, 528:544], in0=sb["basesrow"][0:1, 528:544],
            in1=bS2[0:1, 0:16], op=ALU.add))(bS2), waits=(p.tok(r4), tsnap), label=f"mo.cadd.{call}.{c}")
        cnt_mm_tok = p.tok(radd)
        bk.done(1, cnt_mm_tok)
        locc0 = route[:, c * 10 + 4:c * 10 + 5]
        locc1 = route[:, c * 10 + 5:c * 10 + 6]
        r = p.op("vector", (lambda oh0, bT, tmp8, locc0: lambda e: e.scalar_tensor_tensor(
            out=tmp8, in0=oh0, scalar=1.0, in1=bT[:, 0:8], op0=ALU.mult, op1=ALU.mult,
            accum_out=locc0))(oh0, bT, tmp8, locc0), waits=(tT,), label=f"mo.lc0.{call}.{c}")
        r = p.op("vector", (lambda oh1, bT, tmp8, locc1: lambda e: e.scalar_tensor_tensor(
            out=tmp8, in0=oh1, scalar=1.0, in1=bT[:, 8:16], op0=ALU.mult, op1=ALU.mult,
            accum_out=locc1))(oh1, bT, tmp8, locc1), label=f"mo.lc1.{call}.{c}")
        tloc = p.tok(r)
        bk.done(0, tloc)
        b.st["mo.routescr"] = (tloc,)
    bC, fC = bk.get(3)
    r1 = p.op("tensor", (lambda bC: lambda e: e.matmul(
        out=bC[0:16, 0:8], lhsT=sb["hsel"][0:1, 0:16], rhs=sb["basesrow"][0:1, 528:536],
        start=True, stop=False))(bC), waits=(cnt_mm_tok, fC), label=f"mo.h0.{call}")
    r2 = p.op("tensor", (lambda bC: lambda e: e.matmul(
        out=bC[0:16, 0:8], lhsT=sb["hsel"][0:1, 16:32], rhs=sb["basesrow"][0:1, 536:544],
        start=False, stop=True))(bC), label=f"mo.h1.{call}")
    r = p.op("vector", (lambda bC: lambda e: e.tensor_copy(out=sb["ccsb"][:], in_=bC[0:16, 0:8]))(bC),
             waits=(p.tok(r2),), label=f"mo.ccc.{call}")
    tcc = p.tok(r)
    bk.done(3, tcc)
    tin = p.dop("gpsimd", (lambda: lambda e: e.dma_start(out=g[f"CCIN{call}"], in_=sb["ccsb"][:]))(),
        waits=(tcc,), label=f"mo.cin.{call}", slot=f"mo.cc{CC}")
    rcc = p.op("gpsimd", (lambda: lambda e: e.collective_compute(
        "AllReduce", ALU.add, replica_groups=[list(range(8))],
        ins=[g[f"CCIN{call}"].opt()], outs=[g[f"CCOUT{call}"].opt()]))(),
        waits=(tin,), label=f"mo.ar.{call}")
    tar = p.tok(rcc)
    tout = p.dop("gpsimd", (lambda: lambda e: e.dma_start(out=sb["ccsb"][:], in_=g[f"CCOUT{call}"]))(),
        waits=(tar,), label=f"mo.cout.{call}", slot=f"mo.cc{CC}")
    bC, fC = bk.get(3)
    r1 = p.op("tensor", (lambda bC: lambda e: e.matmul(
        out=bC[0:1, 0:8], lhsT=sb["msel"][:, 0:1], rhs=sb["ccsb"][:],
        start=True, stop=False))(bC), waits=(tout, fC), label=f"mo.of0.{call}")
    r2 = p.op("tensor", (lambda bC: lambda e: e.matmul(
        out=bC[0:1, 8:16], lhsT=sb["msel"][:, 1:2], rhs=sb["ccsb"][:],
        start=False, stop=True))(bC), label=f"mo.of1.{call}")
    r = p.op("vector", (lambda bC: lambda e: e.tensor_copy(
        out=sb["basesrow"][0:1, 560:576], in_=bC[0:1, 0:16]))(bC), waits=(p.tok(r2),), label=f"mo.ofc.{call}")
    toffs = p.tok(r)
    bk.done(3, toffs)

    sctoks = []
    for c in range(NCHUNK):
        par = c % 2
        bB, fB = bk.get(3)
        r1 = p.op("tensor", (lambda c, bB: lambda e: e.matmul(
            out=bB[:, 0:16], lhsT=sb["ones_row"][0:1, :], rhs=sb["basesrow"][0:1, c * 16:(c + 1) * 16],
            start=True, stop=False))(c, bB), waits=(toffs, fB), label=f"mo.bc1.{call}.{c}")
        r2 = p.op("tensor", (lambda bB: lambda e: e.matmul(
            out=bB[:, 16:32], lhsT=sb["ones_row"][0:1, :], rhs=sb["basesrow"][0:1, 560:576],
            start=False, stop=False))(bB), label=f"mo.bc2.{call}.{c}")
        r3b = p.op("tensor", (lambda bB: lambda e: e.matmul(
            out=bB[:, 32:40], lhsT=sb["ones_row"][0:1, :], rhs=sb["basesrow"][0:1, 528:536],
            start=False, stop=True))(bB), label=f"mo.bc3.{call}.{c}")
        tbc = p.tok(r3b)
        gatecols = []
        slottoks = []
        for k in range(2):
            ek = route[:, c * 10 + k:c * 10 + k + 1]
            locc = route[:, c * 10 + 4 + k:c * 10 + 5 + k]
            graw = route[:, c * 10 + 2 + k:c * 10 + 3 + k]
            ohk = small[:, 200 + k * 8:208 + k * 8]
            r = p.op("vector", (lambda ek, ohk: lambda e: e.tensor_scalar(
                out=ohk, in0=sb["iota8"][:], scalar1=ek, scalar2=None, op0=ALU.is_equal))(ek, ohk),
                waits=(tbc, b.st.get("mo.p2scr", ())), label=f"mo.oh.{call}.{c}.{k}")
            basek = small[:, 216 + k:217 + k]
            tmp8 = small[:, 218 + k * 8:226 + k * 8]
            r = p.op("vector", (lambda ohk, bB, tmp8, basek, k: lambda e: e.scalar_tensor_tensor(
                out=tmp8, in0=ohk, scalar=1.0, in1=bB[:, k * 8:(k + 1) * 8], op0=ALU.mult, op1=ALU.mult,
                accum_out=basek))(ohk, bB, tmp8, basek, k), label=f"mo.bse.{call}.{c}.{k}")
            offk = small[:, 234 + k:235 + k]
            tmp8b = small[:, 236 + k * 8:244 + k * 8]
            r = p.op("vector", (lambda ohk, bB, tmp8b, offk, k: lambda e: e.scalar_tensor_tensor(
                out=tmp8b, in0=ohk, scalar=1.0, in1=bB[:, 16 + k * 8:16 + (k + 1) * 8], op0=ALU.mult, op1=ALU.mult,
                accum_out=offk))(ohk, bB, tmp8b, offk, k), label=f"mo.off.{call}.{c}.{k}")
            ll = small[:, 252 + k:253 + k]
            r = p.op("vector", (lambda basek, locc, ll: lambda e: e.tensor_tensor(
                out=ll, in0=basek, in1=locc, op=ALU.add))(basek, locc, ll), label=f"mo.ll.{call}.{c}.{k}")
            lgl = small[:, 254 + k:255 + k]
            r = p.op("vector", (lambda ll, offk, lgl: lambda e: e.tensor_tensor(
                out=lgl, in0=ll, in1=offk, op=ALU.add))(ll, offk, lgl), label=f"mo.lg2.{call}.{c}.{k}")
            keep = small[:, 256 + k:257 + k]
            r = p.op("vector", (lambda lgl, keep: lambda e: e.tensor_scalar(
                out=keep, in0=lgl, scalar1=float(CCAP), scalar2=None, op0=ALU.is_lt))(lgl, keep),
                label=f"mo.kp.{call}.{c}.{k}")
            if k == 1:
                tot0 = small[:, 268:269]
                tmp8c = small[:, 270:278]
                r = p.op("vector", (lambda ohk, bB, tmp8c, tot0: lambda e: e.scalar_tensor_tensor(
                    out=tmp8c, in0=ohk, scalar=1.0, in1=bB[:, 32:40], op0=ALU.mult, op1=ALU.mult,
                    accum_out=tot0))(ohk, bB, tmp8c, tot0), label=f"mo.tt0.{call}.{c}")
                bk.done(3, p.tok(r))
                r = p.op("vector", (lambda ll, tot0: lambda e: e.tensor_tensor(
                    out=ll, in0=ll, in1=tot0, op=ALU.add))(ll, tot0), label=f"mo.lla.{call}.{c}")
            keepl = small[:, 258 + k:259 + k]
            r = p.op("vector", (lambda ll, keepl: lambda e: e.tensor_scalar(
                out=keepl, in0=ll, scalar1=float(CL), scalar2=None, op0=ALU.is_lt))(ll, keepl),
                label=f"mo.kl.{call}.{c}.{k}")
            gk = small[:, 260 + k:261 + k]
            r = p.op("vector", (lambda graw, keep, gk: lambda e: e.tensor_tensor(
                out=gk, in0=graw, in1=keep, op=ALU.mult))(graw, keep, gk), label=f"mo.gk.{call}.{c}.{k}")
            r = p.op("vector", (lambda gk, keepl: lambda e: e.tensor_tensor(
                out=gk, in0=gk, in1=keepl, op=ALU.mult))(gk, keepl), label=f"mo.gk2.{call}.{c}.{k}")
            gatecols.append(gk)
            sl = small[:, 262 + k:263 + k]
            r = p.op("vector", (lambda ek, ll, sl: lambda e: e.scalar_tensor_tensor(
                out=sl, in0=ek, scalar=float(CL), in1=ll, op0=ALU.mult, op1=ALU.add))(ek, ll, sl),
                label=f"mo.sl.{call}.{c}.{k}")
            r = p.op("vector", (lambda sl: lambda e: e.tensor_tensor(
                out=sl, in0=sl, in1=sb["capcol"][:], op=ALU.min))(sl), label=f"mo.slc.{call}.{c}.{k}")
            r = p.op("vector", (lambda sl, c, k: lambda e: e.tensor_copy(
                out=ridx[:, c * 2 + k:c * 2 + k + 1], in_=sl))(sl, c, k), label=f"mo.sli.{call}.{c}.{k}")
            slottoks.append(p.tok(r))
        gsum = small[:, 264:265]
        r = p.op("vector", (lambda gatecols, gsum: lambda e: e.tensor_tensor(
            out=gsum, in0=gatecols[0], in1=gatecols[1], op=ALU.add))(gatecols, gsum), label=f"mo.gs.{call}.{c}")
        r = p.op("vector", (lambda gsum: lambda e: e.tensor_scalar(
            out=gsum, in0=gsum, scalar1=1e-9, scalar2=None, op0=ALU.max))(gsum), label=f"mo.gm.{call}.{c}")
        grs = small[:, 265:266]
        r = p.op("vector", (lambda gsum, grs: lambda e: e.reciprocal(out=grs, in_=gsum))(gsum, grs),
                 label=f"mo.gr2.{call}.{c}")
        tgr = p.tok(r)
        for k in range(2):
            r = p.op("vector", (lambda gk, grs, c, k: lambda e: e.tensor_scalar(
                out=route[:, c * 10 + 6 + k:c * 10 + 7 + k], in0=gk, scalar1=grs, scalar2=None,
                op0=ALU.mult))(gatecols[k], grs, c, k), waits=(tgr,), label=f"mo.gf.{call}.{c}.{k}")
        tgf = p.tok(r)
        b.st["mo.p2scr"] = (tgf,)
        lxb = stream[:, par * 1024:par * 1024 + 1024]
        tld = p.dop("sync", (lambda c, lxb: lambda e: e.dma_start(
            out=lxb, in_=g["LNX"][c * 128:(c + 1) * 128, :]))(c, lxb),
            waits=b.st.get(f"mo.lxbfree{par}", ()), label=f"mo.sld.{call}.{c}", slot=f"mo.sld{CC}{par}")
        sc2 = []
        for k in range(2):
            t = p.dop("gpsimd", (lambda c, k, lxb: lambda e: e.indirect_dma_start(
                out=g["XBUF"], out_offset=IndirectOffsetOnAxis(ap=ridx[:, c * 2 + k:c * 2 + k + 1], axis=0),
                in_=lxb, in_offset=None))(c, k, lxb),
                waits=(tld, slottoks[k], b.st.get("mo.scprev", ())) + (tuple(b.st["pre"][-1:]) if c == 0 and call == 0 else ()),
                label=f"mo.sc.{call}.{c}.{k}", slot="mo.sc")
            b.st["mo.scprev"] = (t,)
            sc2.append(t)
        b.st[f"mo.lxbfree{par}"] = tuple(sc2)
        sctoks += sc2

    if os.environ.get("KDBG_MO") == str(call):
        rdbg = p.op("vector", lambda e: e.tensor_copy(out=sb["small"][:, 440:504], in_=sb["ridx"][:, 0:64]),
                    waits=tuple(sctoks), label="mo.dbgcast")
        p.dop("scalar", lambda e: e.dma_start(out=g["DBG"][0:128, 0:320], in_=sb["route"][:]),
              waits=tuple(sctoks) + b.st.get("mo.p2scr", ()), label="mo.dbg1", slot="mo.dbgs")
        p.dop("scalar", lambda e: e.dma_start(out=g["DBG"][0:128, 320:384], in_=sb["small"][:, 440:504]),
              waits=(p.tok(rdbg),), label="mo.dbg2", slot="mo.dbgs")
        p.dop("scalar", lambda e: e.dma_start(out=g["DBG"][0:1, 384:960], in_=sb["basesrow"][0:1, :]),
              waits=tuple(sctoks), label="mo.dbg3", slot="mo.dbgs")
        p.dop("scalar", lambda e: e.dma_start(out=g["DBG"][0:16, 960:968], in_=sb["ccsb"][:]),
              waits=tuple(sctoks), label="mo.dbg4", slot="mo.dbgs")
    ob_toks = []
    for ex in range(EXP):
        xg = am[:, 0:12288].rearrange("p (m t) -> p m t", m=8)
        txg = p.dop("sync", (lambda ex: lambda e: e.dma_start_transpose(
            out=xg[:, :, :], in_=g["XBUF"][ex * CL:(ex + 1) * CL, :]))(ex),
            waits=tuple(sctoks) + b.st.get("mo.xgfree", ()), label=f"mo.xg.{call}.{ex}", slot=f"mo.xg{CC}")
        wtoks = []
        for d in range(8):
            t = p.dop("gpsimd", (lambda ex, d: lambda e: e.dma_start(
                out=aw0[:, d * 2048:(d + 1) * 2048], in_=g["wgin"][ex, d * 128:(d + 1) * 128, 0:2048]))(ex, d),
                waits=b.st.get("mo.wafree", ()), label=f"mo.wa.{call}.{ex}.{d}", slot=f"mo.wa{CC}")
            wtoks.append(t)
        geltoks = []
        hidfree = b.st.get("mo.hidfree", ())
        for f in range(16):
            ib = 4 + f % 2
            bank, bfree = bk.get(ib)
            last = None
            for ch in range(3):
                for d in range(8):
                    last = p.op("tensor", (lambda f, ch, d, bank: lambda e: e.matmul(
                        out=bank[:, 0:512],
                        lhsT=aw0[:, d * 2048 + f * 128:d * 2048 + (f + 1) * 128],
                        rhs=xg[:, d, ch * 512:(ch + 1) * 512],
                        start=(d == 0), stop=(d == 7)))(f, ch, d, bank),
                        waits=(wtoks, txg, ((bfree,) if d == 0 else ())),
                        label=f"mo.ha.{call}.{ex}.{f}.{ch}.{d}")
                th = p.tok(last)
                r = p.op("scalar", (lambda f, ch, bank: lambda e: e.activation(
                    out=aw1[:, f * 1536 + ch * 512:f * 1536 + (ch + 1) * 512], in_=bank[:, 0:512],
                    func=AF.Gelu))(f, ch, bank),
                    waits=(th, hidfree), label=f"mo.ga.{call}.{ex}.{f}.{ch}")
                tg = p.tok(r)
                bk.done(ib, tg)
                bfree = tg
                geltoks.append(tg)
        wtoks2 = []
        for d in range(8):
            t = p.dop("gpsimd", (lambda ex, d: lambda e: e.dma_start(
                out=aw0[:, d * 2048:(d + 1) * 2048], in_=g["wgin"][ex, d * 128:(d + 1) * 128, 2048:4096]))(ex, d),
                waits=(geltoks[-1],), label=f"mo.wg2.{call}.{ex}.{d}", slot=f"mo.wa{CC}")
            wtoks2.append(t)
        multoks = []
        for f in range(16):
            ib = 4 + f % 2
            bank, bfree = bk.get(ib)
            last = None
            for ch in range(3):
                for d in range(8):
                    last = p.op("tensor", (lambda f, ch, d, bank: lambda e: e.matmul(
                        out=bank[:, 0:512],
                        lhsT=aw0[:, d * 2048 + f * 128:d * 2048 + (f + 1) * 128],
                        rhs=xg[:, d, ch * 512:(ch + 1) * 512],
                        start=(d == 0), stop=(d == 7)))(f, ch, d, bank),
                        waits=(wtoks2, ((bfree,) if d == 0 else ())),
                        label=f"mo.hg.{call}.{ex}.{f}.{ch}.{d}")
                th = p.tok(last)
                r = p.op("vector", (lambda f, ch, bank: lambda e: e.tensor_tensor(
                    out=aw1[:, f * 1536 + ch * 512:f * 1536 + (ch + 1) * 512],
                    in0=aw1[:, f * 1536 + ch * 512:f * 1536 + (ch + 1) * 512],
                    in1=bank[:, 0:512], op=ALU.mult))(f, ch, bank),
                    waits=(th,), label=f"mo.gm2.{call}.{ex}.{f}.{ch}")
                tg = p.tok(r)
                bk.done(ib, tg)
                bfree = tg
                multoks.append(tg)
        wotoks = []
        for fb in range(4):
            t = p.dop("gpsimd", (lambda ex, fb: lambda e: e.dma_start(
                out=aw0[:, 16384 + fb * 4096:16384 + (fb + 1) * 4096].rearrange("p (q n) -> p q n", q=4),
                in_=g["wgout"][ex, fb * 512:(fb + 1) * 512, :].rearrange("(q r) n -> r q n", q=4)))(ex, fb),
                waits=b.st.get("mo.wofree", ()), label=f"mo.wo.{call}.{ex}.{fb}", slot=f"mo.wo{CC}")
            wotoks.append(t)
        for sc in range(12):
            otoks = []
            for half in range(2):
                ib = 6 + half
                bank, bfree = bk.get(ib)
                last = None
                for ft in range(16):
                    last = p.op("tensor", (lambda sc, half, ft, bank: lambda e: e.matmul(
                        out=bank[:, 0:512],
                        lhsT=aw1[:, ft * 1536 + sc * 128:ft * 1536 + sc * 128 + 128],
                        rhs=aw0[:, 16384 + ft * 1024 + half * 512:16384 + ft * 1024 + half * 512 + 512],
                        start=(ft == 0), stop=(ft == 15)))(sc, half, ft, bank),
                        waits=(multoks, wotoks, ((bfree,) if ft == 0 else ())),
                        label=f"mo.om.{call}.{ex}.{sc}.{half}.{ft}")
                tO = p.tok(last)
                par2 = sc % 2
                obuf = stream[:, 2048 + par2 * 1024 + 0:2048 + par2 * 1024 + 1024]
                r = p.op("vector", (lambda obuf, half, bank: lambda e: e.tensor_copy(
                    out=obuf[:, half * 512:(half + 1) * 512], in_=bank[:, 0:512]))(obuf, half, bank),
                    waits=(tO, b.st.get(f"mo.obfree{sc % 2}", ())), label=f"mo.oe.{call}.{ex}.{sc}.{half}")
                te = p.tok(r)
                bk.done(ib, te)
                otoks.append(te)
            tst = p.dop("scalar", (lambda ex, sc, obuf: lambda e: e.dma_start(
                out=g["OBUF"][ex * CL + sc * 128:ex * CL + (sc + 1) * 128, :], in_=obuf))(ex, sc, stream[:, 2048 + (sc % 2) * 1024:2048 + (sc % 2) * 1024 + 1024]),
                waits=tuple(otoks), label=f"mo.os.{call}.{ex}.{sc}", slot=f"mo.os{CC}{sc % 2}")
            b.st[f"mo.obfree{sc % 2}"] = (tst,)
            ob_toks.append(tst)
        b.st["mo.xgfree"] = (p.tok(last),)
        b.st["mo.wafree"] = (geltoks[-1], multoks[-1])
        b.st["mo.wofree"] = (p.tok(last),)
        b.st["mo.hidfree"] = (p.tok(last),)

    scr_free = {0: (), 1: ()}
    final = (call == 1)
    outtoks = []
    for c in range(NCHUNK):
        par = c % 2
        gb0 = stream[:, 4096 + par * 1024:4096 + par * 1024 + 1024]
        gb1 = am[:, 12288 + par * 1024:12288 + par * 1024 + 1024]
        t0_ = p.dop("gpsimd", (lambda c, gb0: lambda e: e.indirect_dma_start(
            out=gb0, out_offset=None, in_=g["OBUF"],
            in_offset=IndirectOffsetOnAxis(ap=ridx[:, c * 2:c * 2 + 1], axis=0)))(c, gb0),
            waits=tuple(ob_toks) + b.st.get(f"mo.g0free{par}", ()), label=f"mo.g0.{call}.{c}", slot=f"mo.g0{CC}{par}")
        t1_ = p.dop("gpsimd", (lambda c, gb1: lambda e: e.indirect_dma_start(
            out=gb1, out_offset=None, in_=g["OBUF"],
            in_offset=IndirectOffsetOnAxis(ap=ridx[:, c * 2 + 1:c * 2 + 2], axis=0)))(c, gb1),
            waits=tuple(ob_toks) + b.st.get(f"mo.g1free{par}", ()), label=f"mo.g1.{call}.{c}", slot=f"mo.g1{CC}{par}")
        rb = stream[:, par * 1024:par * 1024 + 1024]
        trb = p.dop("sync", (lambda c, rb: lambda e: e.dma_start(
            out=rb, in_=g["RES"][c * 128:(c + 1) * 128, :]))(c, rb),
            waits=b.st.get(f"mo.rbfree{par}", ()), label=f"mo.rld.{call}.{c}", slot=f"mo.rld{CC}{par}")
        xnew = stream[:, 2048 + par * 1024:2048 + par * 1024 + 1024]
        r = p.op("vector", (lambda c, xnew, gb0: lambda e: e.tensor_scalar(
            out=xnew, in0=gb0, scalar1=route[:, c * 10 + 6:c * 10 + 7], scalar2=None,
            op0=ALU.mult))(c, xnew, gb0), waits=(t0_, b.st.get(f"mo.xnfree{par}", ())), label=f"mo.cb1.{call}.{c}")
        r = p.op("vector", (lambda c, xnew, gb1: lambda e: e.scalar_tensor_tensor(
            out=xnew, in0=gb1, scalar=route[:, c * 10 + 7:c * 10 + 8], in1=xnew,
            op0=ALU.mult, op1=ALU.add))(c, xnew, gb1), waits=(t1_,), label=f"mo.cb2.{call}.{c}")
        r = p.op("vector", (lambda xnew, rb: lambda e: e.tensor_tensor(
            out=xnew, in0=xnew, in1=rb, op=ALU.add))(xnew, rb), waits=(trb,), label=f"mo.cb3.{call}.{c}")
        b.st[f"mo.g0free{par}"] = (p.tok(r),)
        b.st[f"mo.g1free{par}"] = (p.tok(r),)
        b.st[f"mo.rbfree{par}"] = (p.tok(r),)
        txn = p.tok(r)
        if final:
            fo = sb["fout"][:, 0:1024]
            r = p.op("vector", (lambda fo, xnew: lambda e: e.tensor_copy(out=fo, in_=xnew))(fo, xnew),
                     waits=(txn,) + b.st.get("mo.fofree", ()), label=f"mo.fc.{call}.{c}")
            tst = p.dop("gpsimd", (lambda c, fo: lambda e: e.dma_start(
                out=g["out"][c * 128:(c + 1) * 128, :], in_=fo))(c, fo),
                waits=(p.tok(r),), label=f"mo.fst.{call}.{c}", slot=f"mo.fst{par}")
            b.st["mo.fofree"] = (tst,)
            b.st[f"mo.xnfree{par}"] = (tst,)
            outtoks.append(tst)
        else:
            mcol = stat[:, 16 + par:17 + par]
            rcol = stat[:, 18 + par:19 + par]
            stoks, sfree = ln_cols(b, xnew, sb["epsc"][:, 0:1], mcol, rcol,
                                   (txn,) + scr_free[par], 80 + par * 16, f"mo.l.{call}.{c}")
            scr_free[par] = tuple(sfree)
            lnx = stream[:, 4096 + par * 1024:4096 + par * 1024 + 1024]
            r = p.op("vector", (lambda lnx, xnew, mcol, rcol: lambda e: e.tensor_scalar(
                out=lnx, in0=xnew, scalar1=mcol, scalar2=rcol, op0=ALU.subtract, op1=ALU.mult))(lnx, xnew, mcol, rcol),
                waits=tuple(stoks), label=f"mo.lna.{call}.{c}")
            tl = p.tok(r)
            t1 = p.dop("scalar", (lambda c, xnew: lambda e: e.dma_start(
                out=g["RES"][c * 128:(c + 1) * 128, :], in_=xnew))(c, xnew),
                waits=(tl,), label=f"mo.sr.{call}.{c}", slot=f"mo.sr{par}")
            t2 = p.dop("scalar", (lambda c, lnx: lambda e: e.dma_start(
                out=g["LNX"][c * 128:(c + 1) * 128, :], in_=lnx))(c, lnx),
                waits=(tl,), label=f"mo.sxx.{call}.{c}", slot=f"mo.sr{par}")
            b.st[f"mo.xnfree{par}"] = (t1, t2)
            b.st[f"mo.g0free{par}"] = b.st[f"mo.g0free{par}"] + (t2,)
    if final:
        p.op("sync", None, waits=outtoks, label="mo.finwait")
    b.st["mo.lxfree"] = ()
    b.st["mo.xgfree"] = ()
    b.st["mo.wafree"] = ()
    b.st["mo.wofree"] = ()
    b.st["mo.hidfree"] = ()


def build_mamba(b):
    p, g, sb, ps = b.p, b.g, b.sb, b.ps
    aw0, aw1, am, stream = sb["aw0"], sb["aw1"], sb["am"], sb["stream"]
    bk = Banks(ps)
    wsdone = []
    stage_par = [()] * 2
    bands = []
    for w, ws, nb, wcols in [("inpw", "WS_inp", 32, 1024), ("dtw", "WS_dt", 16, 2048),
                             ("bw", "WS_b", 16, 2048), ("cw", "WS_c", 16, 2048), ("outw", "WS_out", 8, 2048)]:
        for i in range(nb):
            bands.append((w, ws, i, wcols))
    for bi_, (w, ws, i, wcols) in enumerate(bands):
        par = bi_ % 2
        buf = stream[:, par * 2048:par * 2048 + 2048]
        t1 = p.dop("gpsimd", (lambda w, i, wcols, buf: lambda e: e.dma_start(
            out=buf[:, 0:wcols], in_=g[w][i * 128:(i + 1) * 128, :]))(w, i, wcols, buf),
            waits=stage_par[par], label=f"ma.wl.{bi_}", slot=f"ma.wl{par}")
        t2 = p.dop("scalar", (lambda ws, i, wcols, buf: lambda e: e.dma_start(
            out=g[ws][i * 128:(i + 1) * 128, 0:wcols], in_=buf[:, 0:wcols]))(ws, i, wcols, buf),
            waits=(t1,), label=f"ma.ws.{bi_}", slot=f"ma.ws{par}")
        stage_par[par] = (t2,)
        wsdone.append(t2)

    for bi in range(NB):
        T0 = bi * SEQ
        lnxT = am[:, 0:16384].rearrange("p (m t) -> p m t", m=8)
        a1 = []
        for c in range(16):
            t = p.dop("sync", (lambda c, T0: lambda e: e.dma_start_transpose(
                out=lnxT[:, :, c * 128:(c + 1) * 128],
                in_=g["LNX"][T0 + c * 128:T0 + (c + 1) * 128, :]))(c, T0),
                waits=b.st.get("ma.lnxfree", ()), label=f"ma.lx.{bi}.{c}", slot="ma.lx")
            a1.append(t)
        zstoks = []
        for f in range(32):
            par = f % 2
            wst = stream[:, 4096 + par * 1024:4096 + par * 1024 + 1024]
            tws = p.dop("sync", (lambda f, wst: lambda e: e.dma_start_transpose(
                out=wst.rearrange("p (m t) -> p m t", m=8),
                in_=g["WS_inp"][f * 128:(f + 1) * 128, 0:1024]))(f, wst),
                waits=(wsdone[f], b.st.get(f"ma.wstfree{par}", ())), label=f"ma.wi.{bi}.{f}", slot=f"ma.wst{par}")
            isz = f >= 16
            lastmm = None
            chtoks = []
            for ch in range(4):
                ib = ch % 2
                bank, bfree = bk.get(ib)
                for d in range(8):
                    lastmm = p.op("tensor", (lambda f, ch, d, bank, wst: lambda e: e.matmul(
                        out=bank[:, 0:512], lhsT=wst[:, d * 128:(d + 1) * 128],
                        rhs=lnxT[:, d, ch * 512:(ch + 1) * 512],
                        start=(d == 0), stop=(d == 7)))(f, ch, d, bank, wst),
                        waits=(tws, a1, (bfree if d == 0 else ())),
                        label=f"ma.xz.{bi}.{f}.{ch}.{d}")
                tmm = p.tok(lastmm)
                if not isz:
                    r = p.op("vector", (lambda f, ch, bank: lambda e: e.tensor_scalar(
                        out=aw0[:, f * 2048 + ch * 512:f * 2048 + (ch + 1) * 512], in0=bank[:, 0:512],
                        scalar1=sb["inpb"][:, f:f + 1], scalar2=None, op0=ALU.add))(f, ch, bank),
                        waits=(tmm, (b.st.get("ma.aw0free", ()) if f == 0 and ch == 0 else ())),
                        label=f"ma.xe.{bi}.{f}.{ch}")
                else:
                    zb = stream[:, (f % 2) * 2048:(f % 2) * 2048 + 2048]
                    r = p.op("vector", (lambda f, ch, bank, zb: lambda e: e.tensor_scalar(
                        out=zb[:, ch * 512:(ch + 1) * 512], in0=bank[:, 0:512],
                        scalar1=sb["inpb"][:, f:f + 1], scalar2=None, op0=ALU.add))(f, ch, bank, zb),
                        waits=(tmm,) + ((b.st.get(f"ma.zbfree{f % 2}", ()) + tuple(stage_par)
                                         + b.st.get("ma.xnfree0", ()) + b.st.get("ma.xnfree1", ())
                                         + b.st.get("ma.rbfree0", ()) + b.st.get("ma.rbfree1", ())) if ch == 0 else ()),
                        label=f"ma.ze.{bi}.{f}.{ch}")
                te = p.tok(r)
                bk.done(ib, te)
                chtoks.append(te)
            b.st[f"ma.wstfree{par}"] = (chtoks[-1],)
            if isz:
                tz = p.dop("scalar", (lambda f, T0, zb: lambda e: e.dma_start(
                    out=g["ZSCR"][(f - 16) * 128:(f - 15) * 128, T0:T0 + 2048], in_=zb))(f, T0, stream[:, (f % 2) * 2048:(f % 2) * 2048 + 2048]),
                    waits=tuple(chtoks), label=f"ma.zs.{bi}.{f}", slot=f"ma.zs{f % 2}")
                b.st[f"ma.zbfree{f % 2}"] = (tz,)
                zstoks.append(tz)
        convtoks = []
        for f in range(16):
            acc = am[:, 0:2048]
            xsl = aw0[:, f * 2048:(f + 1) * 2048]
            w = (p.tok(lastmm),) if f == 0 else ()
            r = p.op("vector", (lambda acc, xsl, f: lambda e: e.tensor_scalar(
                out=acc, in0=xsl, scalar1=sb["convw"][:, f * 4 + 3:f * 4 + 4], scalar2=None,
                op0=ALU.mult))(acc, xsl, f), waits=w + b.st.get("ma.accfree", ()), label=f"ma.cv3.{bi}.{f}")
            for tap in range(2, -1, -1):
                sh = 3 - tap
                r = p.op("vector", (lambda acc, xsl, f, tap, sh: lambda e: e.scalar_tensor_tensor(
                    out=acc[:, sh:2048], in0=xsl[:, 0:2048 - sh], scalar=sb["convw"][:, f * 4 + tap:f * 4 + tap + 1],
                    in1=acc[:, sh:2048], op0=ALU.mult, op1=ALU.add))(acc, xsl, f, tap, sh),
                    label=f"ma.cv{tap}.{bi}.{f}")
            tacc = p.tok(r)
            r = p.op("scalar", (lambda acc, xsl, f: lambda e: e.activation(
                out=xsl, in_=acc, func=AF.Silu, bias=sb["convb"][:, f:f + 1]))(acc, xsl, f),
                waits=(tacc,), label=f"ma.si.{bi}.{f}")
            tu = p.tok(r)
            b.st["ma.accfree"] = (tu,)
            convtoks.append(tu)
        if os.environ.get("KDBG_MA") and bi == 0:
            tdbg = p.dop("scalar", lambda e: e.dma_start(
                out=g["ATTN"].rearrange("(p q) d -> p (q d)", p=128), in_=aw0[:, 0:32768]),
                waits=tuple(convtoks), label="ma.dbgu", slot="ma.dbg")
            b.st["ma.aw0free"] = b.st.get("ma.aw0free", ()) + (tdbg,)
        ytoks = []
        for f in range(16):
            bufs = [am[:, (1 + i) * 2048:(2 + i) * 2048] for i in range(5)]
            projtoks = {}
            for wi, (wsname, bslot) in enumerate([("WS_dt", 0), ("WS_b", 2), ("WS_c", 3)]):
                par = wi % 2
                wst = stream[:, par * 2048:par * 2048 + 2048]
                tws = p.dop("sync", (lambda wsname, f, wst: lambda e: e.dma_start_transpose(
                    out=wst.rearrange("p (m t) -> p m t", m=16),
                    in_=g[wsname][f * 128:(f + 1) * 128, 0:2048]))(wsname, f, wst),
                    waits=(b.st.get(f"ma.wst2free{par}", ()), b.st.get(f"ma.zbfree{par}", ()), wsdone[-1], wsdone[-2]), label=f"ma.w{wi}.{bi}.{f}", slot=f"ma.wst2{par}")
                chtoks = []
                for ch in range(4):
                    ib = 4 + ch % 2
                    bank, bfree = bk.get(ib)
                    last = None
                    for k in range(16):
                        last = p.op("tensor", (lambda ch, k, bank, wst: lambda e: e.matmul(
                            out=bank[:, 0:512], lhsT=wst.rearrange("p (m t) -> p m t", m=16)[:, k, :],
                            rhs=aw0[:, k * 2048 + ch * 512:k * 2048 + (ch + 1) * 512],
                            start=(k == 0), stop=(k == 15)))(ch, k, bank, wst),
                            waits=(tws, convtoks, (bfree if k == 0 else ())),
                            label=f"ma.p{wi}.{bi}.{f}.{ch}.{k}")
                    tmm = p.tok(last)
                    if wi == 0:
                        r = p.op("scalar", (lambda ch, bank, bufs, f: lambda e: e.activation(
                            out=bufs[0][:, ch * 512:(ch + 1) * 512], in_=bank[:, 0:512], func=AF.Exp,
                            bias=sb["dtb"][:, f:f + 1]))(ch, bank, bufs, f),
                            waits=(tmm, (b.st.get("ma.b1free", ()) if ch == 0 else ())), label=f"ma.dte.{bi}.{f}.{ch}")
                    elif wi == 1:
                        r = p.op("scalar", (lambda ch, bank, bufs: lambda e: e.activation(
                            out=bufs[2][:, ch * 512:(ch + 1) * 512], in_=bank[:, 0:512], func=AF.Tanh))(ch, bank, bufs),
                            waits=(tmm, (b.st.get("ma.b3free", ()) if ch == 0 else ())), label=f"ma.be.{bi}.{f}.{ch}")
                    else:
                        r = p.op("scalar", (lambda ch, bank, bufs: lambda e: e.activation(
                            out=bufs[3][:, ch * 512:(ch + 1) * 512], in_=bank[:, 0:512], func=AF.Tanh))(ch, bank, bufs),
                            waits=(tmm, (b.st.get("ma.b4free", ()) if ch == 0 else ())), label=f"ma.ce.{bi}.{f}.{ch}")
                    te = p.tok(r)
                    bk.done(ib, te)
                    chtoks.append(te)
                projtoks[wi] = chtoks
                b.st[f"ma.wst2free{par}"] = (chtoks[-1],)
            r = p.op("scalar", (lambda bufs: lambda e: e.activation(
                out=bufs[0][:], in_=bufs[0][:], func=AF.Ln, bias=1.0))(bufs),
                waits=tuple(projtoks[0]), label=f"ma.ln1p.{bi}.{f}")
            tln1p = p.tok(r)
            r = p.op("vector", (lambda bufs, f: lambda e: e.tensor_scalar(
                out=bufs[0][:], in0=bufs[0][:], scalar1=1e-4, scalar2=sb["acol"][:, f:f + 1],
                op0=ALU.add, op1=ALU.mult))(bufs, f), waits=(tln1p,), label=f"ma.a.{bi}.{f}")
            ta = p.tok(r)
            r = p.op("scalar", (lambda bufs: lambda e: e.activation(
                out=bufs[1][:], in_=bufs[0][:], func=AF.Exp, scale=-1.0))(bufs),
                waits=(ta, b.st.get("ma.b2free", ())), label=f"ma.dec.{bi}.{f}")
            tdec = p.tok(r)
            r = p.op("vector", (lambda bufs, f: lambda e: e.tensor_tensor(
                out=bufs[2][:], in0=bufs[2][:], in1=aw0[:, f * 2048:(f + 1) * 2048], op=ALU.mult))(bufs, f),
                waits=tuple(projtoks[1]), label=f"ma.bu.{bi}.{f}")
            r = p.op("vector", (lambda bufs: lambda e: e.tensor_tensor(
                out=bufs[0][:], in0=bufs[1][:], in1=bufs[2][:], op=ALU.mult))(bufs),
                waits=(tdec,), label=f"ma.dbu.{bi}.{f}")
            r = p.op("vector", (lambda bufs: lambda e: e.tensor_tensor(
                out=bufs[2][:], in0=bufs[2][:], in1=bufs[0][:], op=ALU.subtract))(bufs),
                label=f"ma.inc.{bi}.{f}")
            tinc = p.tok(r)
            r = p.op("vector", (lambda bufs: lambda e: e.tensor_tensor_scan(
                out=bufs[0][:], data0=bufs[1][:], data1=bufs[2][:], initial=0.0,
                op0=ALU.mult, op1=ALU.add))(bufs), waits=(tinc,), label=f"ma.scan.{bi}.{f}")
            tscan = p.tok(r)
            b.st["ma.b2free"] = (tscan,)
            b.st["ma.b3free"] = (tscan,)
            r = p.op("vector", (lambda bufs: lambda e: e.tensor_tensor(
                out=bufs[3][:], in0=bufs[3][:], in1=bufs[0][:], op=ALU.mult))(bufs),
                waits=tuple(projtoks[2]) + (tscan,), label=f"ma.cs.{bi}.{f}")
            b.st["ma.b1free"] = (p.tok(r),)
            r = p.op("vector", (lambda bufs, f: lambda e: e.scalar_tensor_tensor(
                out=aw1[:, f * 2048:(f + 1) * 2048], in0=aw0[:, f * 2048:(f + 1) * 2048],
                scalar=sb["dpar"][:, f:f + 1], in1=bufs[3][:], op0=ALU.mult, op1=ALU.add))(bufs, f),
                waits=((b.st.get("ma.aw1free", ()) + b.st.get("ma.dbgyfree", ()) if f == 0 else ()), convtoks[f]), label=f"ma.y.{bi}.{f}")
            ty = p.tok(r)
            b.st["ma.b4free"] = (ty,)
            tz = p.dop("sync", (lambda f, T0, bufs: lambda e: e.dma_start(
                out=bufs[4][:], in_=g["ZSCR"][f * 128:(f + 1) * 128, T0:T0 + 2048]))(f, T0, bufs),
                waits=tuple(zstoks) + b.st.get("ma.b5free", ()), label=f"ma.zl.{bi}.{f}", slot="ma.zl")
            r = p.op("scalar", (lambda bufs: lambda e: e.activation(
                out=bufs[4][:], in_=bufs[4][:], func=AF.Sigmoid))(bufs), waits=(tz,), label=f"ma.sg.{bi}.{f}")
            r = p.op("vector", (lambda bufs, f: lambda e: e.tensor_tensor(
                out=aw1[:, f * 2048:(f + 1) * 2048], in0=aw1[:, f * 2048:(f + 1) * 2048],
                in1=bufs[4][:], op=ALU.mult))(bufs, f), waits=(ty, p.tok(r)), label=f"ma.yg.{bi}.{f}")
            tyg = p.tok(r)
            b.st["ma.b5free"] = (tyg,)
            ytoks.append(tyg)
        if os.environ.get("KDBG_MA") and bi == 0:
            tdbg = p.dop("scalar", lambda e: e.dma_start(
                out=g["XRES"].rearrange("(p q) d -> p (q d)", p=128), in_=aw1[:, 0:32768]),
                waits=tuple(ytoks), label="ma.dbgy", slot="ma.dbg2")
            b.st["ma.dbgyfree"] = (tdbg,)
        t = p.dop("sync", lambda e: e.dma_start_transpose(
            out=am[:, 0:16384].rearrange("p (m t) -> p m t", m=16),
            in_=g["WS_out"][0:1024, 0:2048]),
            waits=(wsdone[-1], wsdone[-2], b.st.get("ma.amofree", ()), (p.tok(ytoks[-1].rec),)),
            label=f"ma.ow.{bi}", slot="ma.ow")
        owtoks = [t]
        if os.environ.get("KDBG_MA") and bi == 0:
            tdbg = p.dop("scalar", lambda e: e.dma_start(
                out=g["OBUF"][0:2048, :].rearrange("(p q) d -> p (q d)", p=128), in_=sb["am"][:, 0:16384]),
                waits=tuple(owtoks), label="ma.dbgw", slot="ma.dbg3")
            b.st["ma.lnxfree"] = b.st.get("ma.lnxfree", ()) + (tdbg,)
        scr_free = {0: (), 1: ()}
        lastout = None
        for sc in range(16):
            par = sc % 2
            rb = stream[:, par * 1024:par * 1024 + 1024]
            trb = p.dop("sync", (lambda sc, T0, rb: lambda e: e.dma_start(
                out=rb, in_=g["RES"][T0 + sc * 128:T0 + (sc + 1) * 128, :]))(sc, T0, rb),
                waits=(b.st.get(f"ma.rbfree{par}", ()), (p.tok(ytoks[-1].rec),) if sc < 2 else ()), label=f"ma.rl.{bi}.{sc}", slot=f"ma.rl{par}")
            xnew = stream[:, 2048 + par * 1024:2048 + par * 1024 + 1024]
            etoks = []
            for half in range(2):
                ib = half
                bank, bfree = bk.get(ib)
                last = None
                for k in range(16):
                    last = p.op("tensor", (lambda sc, half, k, bank: lambda e: e.matmul(
                        out=bank[:, 0:512], lhsT=aw1[:, k * 2048 + sc * 128:k * 2048 + sc * 128 + 128],
                        rhs=am[:, k * 1024 + half * 512:k * 1024 + half * 512 + 512],
                        start=(k == 0), stop=(k == 15)))(sc, half, k, bank),
                        waits=(owtoks, ytoks, (bfree if k == 0 else ())),
                        label=f"ma.o.{bi}.{sc}.{half}.{k}")
                lastout = last
                tO = p.tok(last)
                r = p.op("vector", (lambda half, xnew, bank: lambda e: e.tensor_tensor(
                    out=xnew[:, half * 512:(half + 1) * 512], in0=bank[:, 0:512],
                    in1=sb["moutbt"][:, half * 512:(half + 1) * 512], op=ALU.add))(half, xnew, bank),
                    waits=(tO, trb, b.st.get(f"ma.xnfree{par}", ())), label=f"ma.e1.{bi}.{sc}.{half}")
                r = p.op("vector", (lambda half, xnew, rb: lambda e: e.tensor_tensor(
                    out=xnew[:, half * 512:(half + 1) * 512], in0=xnew[:, half * 512:(half + 1) * 512],
                    in1=rb[:, half * 512:(half + 1) * 512], op=ALU.add))(half, xnew, rb),
                    label=f"ma.e2.{bi}.{sc}.{half}")
                te = p.tok(r)
                bk.done(ib, te)
                etoks.append(te)
            mcol = sb["stat"][:, 20 + par:21 + par]
            rcol = sb["stat"][:, 22 + par:23 + par]
            stoks, sfree = ln_cols(b, xnew, sb["epsc"][:, 0:1], mcol, rcol,
                                   tuple(etoks) + scr_free[par], 80 + par * 16, f"ma.l.{bi}.{sc}")
            scr_free[par] = tuple(sfree)
            lnx = rb
            r = p.op("vector", (lambda lnx, xnew, mcol, rcol: lambda e: e.tensor_scalar(
                out=lnx, in0=xnew, scalar1=mcol, scalar2=rcol, op0=ALU.subtract, op1=ALU.mult))(lnx, xnew, mcol, rcol),
                waits=tuple(stoks), label=f"ma.lna.{bi}.{sc}")
            tl = p.tok(r)
            t1 = p.dop("scalar", (lambda sc, T0, xnew: lambda e: e.dma_start(
                out=g["RES"][T0 + sc * 128:T0 + (sc + 1) * 128, :], in_=xnew))(sc, T0, xnew),
                waits=(tl,), label=f"ma.sr.{bi}.{sc}", slot=f"ma.sr{par}")
            t2 = p.dop("scalar", (lambda sc, T0, lnx: lambda e: e.dma_start(
                out=g["LNX"][T0 + sc * 128:T0 + (sc + 1) * 128, :], in_=lnx))(sc, T0, lnx),
                waits=(tl,), label=f"ma.sl.{bi}.{sc}", slot=f"ma.sr{par}")
            b.st[f"ma.xnfree{par}"] = (t1, t2)
            b.st[f"ma.rbfree{par}"] = (t1, t2)
        b.st["ma.lnxfree"] = (b.st.get("ma.lnxfree", ())[-1:] if os.environ.get("KDBG_MA") else ()) + (p.tok(lastout),)
        b.st["ma.amofree"] = (p.tok(lastout),)
        b.st["ma.aw0free"] = b.st.get("ma.aw0free", ())[-1:] + (p.tok(ytoks[-1].rec),) if os.environ.get("KDBG_MA") else (p.tok(ytoks[-1].rec),)
        b.st["ma.aw1free"] = (p.tok(lastout),)
        b.st["ma.accfree"] = ()


def build_final(b):
    p, g, sb = b.p, b.g, b.sb
    dump = os.environ.get("KDUMP", "RES")
    outtoks = []
    for c in range(NCHUNK):
        par = c % 2
        buf = sb["stream"][:, par * 1024:par * 1024 + 1024]
        fout = sb["fout"][:, 0:1024]
        tld = p.dop("sync", (lambda buf, c: lambda e: e.dma_start(out=buf, in_=g[dump][c * 128:(c + 1) * 128, :]))(buf, c),
                    waits=b.st.get(f"fin.free{par}", ()), label=f"fin.ld{c}", slot=f"fin.ld{par}")
        r = p.op("vector", (lambda fout, buf: lambda e: e.tensor_copy(out=fout, in_=buf))(fout, buf),
                 waits=(tld,) + b.st.get("fin.foutfree", ()), label=f"fin.cp{c}")
        tst = p.dop("gpsimd", (lambda fout, c: lambda e: e.dma_start(out=g["out"][c * 128:(c + 1) * 128, :], in_=fout))(fout, c),
                    waits=(p.tok(r),), label=f"fin.st{c}", slot=f"fin.st{par}")
        b.st[f"fin.free{par}"] = (tst,)
        b.st["fin.foutfree"] = (tst,)
        outtoks.append(tst)
    p.op("sync", None, waits=outtoks, label="fin.wait")


def build(nc):
    kstop = int(os.environ.get("KSTOP", "99"))
    b = B(nc)
    declare(b)
    sem_cms = [nc.semaphore(f"s_{e}") for e in ENGINES]
    sems = {e: cm.__enter__() for e, cm in zip(ENGINES, sem_cms)}
    preload(b)
    build_ph0(b)
    b.p.barrier("ph0")
    if kstop >= 1:
        build_attn(b)
        b.p.barrier("ph1")
    if kstop >= 2:
        build_ffn(b)
        b.p.barrier("ph2")
    if kstop >= 3:
        build_moe(b, 0)
        b.p.barrier("ph5")
    if kstop >= 4:
        build_mamba(b)
        b.p.barrier("ph6")
    if kstop >= 5:
        build_moe(b, 1)
    else:
        build_final(b)
    slots = sorted(b.p.slot_counts)
    slot_cms = [nc.semaphore(f"q_{i}") for i in range(len(slots))]
    slot_sems = {sl: cm.__enter__() for sl, cm in zip(slots, slot_cms)}
    with nc.Block() as block:
        b.p.finalize_and_emit(block, sems, slot_sems)
    for cm in reversed(slot_cms):
        cm.__exit__(None, None, None)
    for cm in reversed(sem_cms):
        cm.__exit__(None, None, None)
    return nc


def kernel(**inputs):
    x = np.asarray(inputs["L0_attn_a_inputs"])
    weights = {
        "wq": inputs["L0_attn_a_Wq"].reshape(D, D),
        "wk": inputs["L0_attn_a_Wk"].reshape(D, D),
        "wv": inputs["L0_attn_a_Wv"].reshape(D, D),
        "wo": inputs["L0_attn_a_Wo"].reshape(D, D),
        "w1": inputs["L0_attn_a_W1"],
        "w2": inputs["L0_attn_a_W2"],
        "inpw": inputs["L0_attn_b_in_proj_w"],
        "dtw": inputs["L0_attn_b_dt_w"],
        "bw": inputs["L0_attn_b_b_w"],
        "cw": inputs["L0_attn_b_c_w"],
        "outw": inputs["L0_attn_b_out_w"],
        "wg": inputs["L0_ffn_Wg"],
        "wgin": inputs["L0_ffn_W_in"],
        "wgout": inputs["L0_ffn_W_out"],
    }
    weights = {k: np.ascontiguousarray(np.asarray(v, np.float32)) for k, v in weights.items()}
    wconsts = host_weight_consts(inputs)

    nc = build(bass.Bass("TRN2", target_bir_lowering=False))

    in_maps = []
    for c in range(8):
        m = dict(weights)
        m.update(wconsts)
        m.update(host_consts(c))
        m["xs"] = np.ascontiguousarray(x[2 * c:2 * c + 2].reshape(NTOK, D))
        in_maps.append(m)
    res = run_bass_kernel_spmd(nc, in_maps, core_ids=list(range(8)))
    out = np.stack([r["out"].reshape(NB, SEQ, D) for r in res.results], 0).reshape(16, SEQ, D)
    return out
